# version 16
# baseline (speedup 1.0000x reference)
import numpy as np
import contextlib
import concourse.bass as bass
import concourse.mybir as mybir
from concourse.bass_utils import run_bass_kernel_spmd
from concourse.bass_core_rust import IndirectOffsetOnAxis

F32 = mybir.dt.float32
I32 = mybir.dt.int32
AF = mybir.ActivationFunctionType
OP = mybir.AluOpType
AX = mybir.AxisListType

D = 1024
DG = 8
NST = 16
NEXP = 8
HE = 2048
H2 = 4096
TOWN = 2048
HALF = 1024
WARM = 128
NTC = 8
CAP = 384
TRASH = 3072
EOGR = 3136
BIG = 1e30
EPS = 1e-5

TAPS = set()
DBG_SKIP_EXPERTS = False
DBG_SKIP_IDX = False
DBG_SKIP_BIAS = False


def build_program():
    nc = bass.Bass("TRN2", target_bir_lowering=False)

    dx = nc.dram_tensor("x", [TOWN, D], F32, kind="ExternalInput")
    dxw = nc.dram_tensor("xw", [WARM, D], F32, kind="ExternalInput")
    dmask = nc.dram_tensor("mask", [1, 1], F32, kind="ExternalInput")
    dlnw = nc.dram_tensor("ln_w", [D], F32, kind="ExternalInput")
    dlnb = nc.dram_tensor("ln_b", [D], F32, kind="ExternalInput")
    dln2w = nc.dram_tensor("ln2_w", [D], F32, kind="ExternalInput")
    dln2b = nc.dram_tensor("ln2_b", [D], F32, kind="ExternalInput")
    dAlog = nc.dram_tensor("A_log", [D, NST], F32, kind="ExternalInput")
    dB = nc.dram_tensor("B_mat", [D, NST], F32, kind="ExternalInput")
    dC = nc.dram_tensor("C_mat", [D, NST], F32, kind="ExternalInput")
    dD = nc.dram_tensor("D_vec", [D], F32, kind="ExternalInput")
    doutwT = nc.dram_tensor("outwT", [D, D], F32, kind="ExternalInput")
    doutb = nc.dram_tensor("out_b", [D], F32, kind="ExternalInput")
    dffn1T = nc.dram_tensor("ffn1T", [D, H2], F32, kind="ExternalInput")
    dffn1b = nc.dram_tensor("ffn1_b", [H2], F32, kind="ExternalInput")
    dffn2T = nc.dram_tensor("ffn2T", [H2, D], F32, kind="ExternalInput")
    dffn2b = nc.dram_tensor("ffn2_b", [D], F32, kind="ExternalInput")
    dWr = nc.dram_tensor("Wr", [D, NEXP], F32, kind="ExternalInput")
    drb = nc.dram_tensor("rb", [NEXP], F32, kind="ExternalInput")
    dW1 = nc.dram_tensor("W1", [NEXP, D, HE], F32, kind="ExternalInput")
    dW2 = nc.dram_tensor("W2", [NEXP, HE, D], F32, kind="ExternalInput")
    dW3 = nc.dram_tensor("W3", [NEXP, D, HE], F32, kind="ExternalInput")
    dWs1 = nc.dram_tensor("Ws1", [D, HE], F32, kind="ExternalInput")
    dWs2 = nc.dram_tensor("Ws2", [HE, D], F32, kind="ExternalInput")
    dWs3 = nc.dram_tensor("Ws3", [D, HE], F32, kind="ExternalInput")
    dident = nc.dram_tensor("c_ident", [128, 128], F32, kind="ExternalInput")
    dxnh = nc.dram_tensor("xnh", [HALF, D], F32)
    dstok = nc.dram_tensor("srctok", [3328], I32)
    deog = nc.dram_tensor("eog", [EOGR, D], F32)
    diota = nc.dram_tensor("c_iota", [NEXP], F32, kind="ExternalInput")
    dout = nc.dram_tensor("out", [TOWN, D], F32, kind="ExternalOutput")
    dtap = {}
    for t in TAPS:
        shp = {"xn": [TOWN, D], "y": [TOWN, D], "x1": [TOWN, D],
               "s": [TOWN, D], "w": [TOWN, NEXP], "acc": [TOWN, D],
               "ys": [TOWN, D], "lg": [TOWN, NEXP], "rx": [128, D],
               "b0": [128, D], "ps": [TOWN, 2], "st": [128, 26]}[t]
        dtap[t] = nc.dram_tensor("tap_" + t, shp, F32, kind="ExternalOutput")

    with contextlib.ExitStack() as ES:
        sbuf = lambda name, shape: ES.enter_context(nc.sbuf_tensor(name, shape, F32))
        ident = sbuf("ident", [128, 128])
        lnw_r = sbuf("lnw_r", [128, D]); lnb_r = sbuf("lnb_r", [128, D])
        ln2w_r = sbuf("ln2w_r", [128, D]); ln2b_r = sbuf("ln2b_r", [128, D])
        ffn1b_c = sbuf("ffn1b_c", [128, 32])
        outb_row = sbuf("outb_row", [1, D]); ffn2b_row = sbuf("ffn2b_row", [1, D])
        ones_row = sbuf("ones_row", [1, 128])
        ones_col = sbuf("ones_col", [128, 1])
        rb_r = sbuf("rb_r", [128, NEXP]); iota_r = sbuf("iota_r", [128, NEXP])
        maskc = sbuf("maskc", [128, 1])
        epsc = sbuf("epsc", [128, 1]); sc32 = sbuf("sc32", [128, 1])
        Asig = sbuf("Asig", [128, 128])
        Bsb = sbuf("Bsb", [128, 128]); Csb = sbuf("Csb", [128, 128])
        Dsb = sbuf("Dsb", [128, DG])
        wr_sb = sbuf("wr_sb", [128, DG * NEXP])
        stcol = sbuf("stcol", [128, 128])
        w_T = sbuf("w_T", [128, NTC * NEXP])
        cols = sbuf("cols", [128, 96])
        r8a = sbuf("r8a", [128, NEXP]); r8b = sbuf("r8b", [128, NEXP])
        r8c = sbuf("r8c", [128, NEXP]); r8d = sbuf("r8d", [128, NEXP])
        M1b = sbuf("M1b", [128, NTC * NEXP]); M2b = sbuf("M2b", [128, NTC * NEXP])
        W1c = sbuf("W1c", [128, NTC]); W2c = sbuf("W2c", [128, NTC])
        P1i = ES.enter_context(nc.sbuf_tensor("P1i", [128, NTC], I32))
        P2i = ES.enter_context(nc.sbuf_tensor("P2i", [128, NTC], I32))
        st_sb = ES.enter_context(nc.sbuf_tensor("st_sb", [128, 26], I32))
        tok_i = ES.enter_context(nc.sbuf_tensor("tok_i", [128, 2], I32))
        ecr = sbuf("ecr", [128, NEXP])
        partcol = sbuf("partcol", [128, 2])
        BUF0 = sbuf("BUF0", [128, 8192])
        BUF1 = sbuf("BUF1", [128, 8192])
        BUF2 = sbuf("BUF2", [128, 8192])
        BUF3 = sbuf("BUF3", [128, 8192])
        BUF4 = sbuf("BUF4", [128, 6144])
        BUFY = sbuf("BUFY", [128, 6144])
        PS = ES.enter_context(nc.psum_tensor("PS", [128, 4096], F32))

        def bank(i, lo=0, n=512):
            return PS[:, i * 512 + lo:i * 512 + lo + n]

        def bank2(i, n=1024):
            return PS[:, i * 512:i * 512 + n]

        mk_sem = lambda name: ES.enter_context(nc.semaphore(name))
        SEM = {k: mk_sem(k) for k in
               ["pT", "pV", "pS", "pG", "dX", "dO"]
               + ["ws%d" % i for i in range(6)] + ["dC%d" % i for i in range(4)]
               + ["dH", "dG", "dE", "dGb", "dL", "dG2", "dG3"]}
        CNT = {k: 0 for k in SEM}

        Q = {"tensor": [], "vector": [], "scalar": [], "gpsimd": [], "sync": []}

        SELF = {"vector": "pV", "scalar": "pS", "gpsimd": "pG"}

        def emit(eng, meth, *args, **kw):
            s = SELF.get(eng)
            if s is not None and meth not in ("dma_start", "indirect_dma_start"):
                if CNT[s] > 0:
                    Q[eng].append(("wait", SEM[s], CNT[s]))
                Q[eng].append(("op", meth, args, kw, [(SEM[s], 1)]))
                CNT[s] += 1
            else:
                Q[eng].append(("op", meth, args, kw, []))

        def wait(eng, semname, val):
            if val > 0:
                Q[eng].append(("wait", SEM[semname], val))

        def mark(eng, semname, amt=1):
            s = SELF.get(eng)
            if s == semname:
                return CNT[semname]
            assert Q[eng] and Q[eng][-1][0] == "op"
            CNT[semname] += amt
            Q[eng][-1][4].append((SEM[semname], amt))
            return CNT[semname]

        def DMASYNC(eng, out_ap, in_ap, semname, **kw):
            return dma(eng, out_ap, in_ap, "ws0", **kw)

        def dma(eng, out_ap, in_ap, semname, **kw):
            if eng == "gpsimd":
                wait(eng, semname, CNT[semname])
            emit(eng, "dma_start", out_ap, in_ap, **kw)
            return mark(eng, semname, 16)

        def ap(handle, offset, pattern):
            return bass.AP(handle, offset, pattern)

        bank_free = {b: ("pV", 0) for b in range(8)}

        def open_banks(*bs):
            for b in bs:
                sem, v = bank_free[b]
                wait("tensor", sem, v)

        def set_free(bs, semname, v):
            for b in bs:
                bank_free[b] = (semname, v)

        yrel = [0] * 6

        def stream_slot(i, src_ap):
            wait("sync", "pT", yrel[i])
            return dma("sync", BUFY[:, i * 1024:i * 1024 + D], src_ap, "ws%d" % i)

        DMASYNC("sync", lnw_r[:, :], ap(dlnw, 0, [[0, 128], [1, D]]), "dX")
        DMASYNC("sync", lnb_r[:, :], ap(dlnb, 0, [[0, 128], [1, D]]), "dX")
        DMASYNC("sync", ln2w_r[:, :], ap(dln2w, 0, [[0, 128], [1, D]]), "dX")
        DMASYNC("sync", ln2b_r[:, :], ap(dln2b, 0, [[0, 128], [1, D]]), "dX")
        dma("scalar", ffn1b_c[:, :], ap(dffn1b, 0, [[1, 128], [128, 32]]), "dX",
            allow_slow_non_contiguous=True)
        dma("scalar", outb_row[:, :], ap(doutb, 0, [[0, 1], [1, D]]), "dX")
        dma("scalar", ffn2b_row[:, :], ap(dffn2b, 0, [[0, 1], [1, D]]), "dX")
        dma("scalar", rb_r[:, :], ap(drb, 0, [[0, 128], [1, NEXP]]), "dX")
        dma("scalar", iota_r[:, :], ap(diota, 0, [[0, 128], [1, NEXP]]), "dX")
        dma("scalar", maskc[:, :], ap(dmask, 0, [[0, 128], [1, 1]]), "dX")
        dma("scalar", ident[:, :], dident.ap(), "dX")
        DMASYNC("sync", BUF0[:, 0:128], ap(dAlog, 0, [[NST, 128], [NST * 128, DG], [1, NST]]), "dX")
        DMASYNC("sync", Bsb[:, :], ap(dB, 0, [[NST, 128], [NST * 128, DG], [1, NST]]), "dX")
        DMASYNC("sync", Csb[:, :], ap(dC, 0, [[NST, 128], [NST * 128, DG], [1, NST]]), "dX")
        dma("scalar", Dsb[:, :], ap(dD, 0, [[1, 128], [128, DG]]), "dX",
            allow_slow_non_contiguous=True)
        CONST_DX = dma("scalar", wr_sb[:, :], ap(dWr, 0, [[NEXP, 128], [NEXP * 128, DG], [1, NEXP]]), "dX")

        emit("gpsimd", "memset", ones_row[:, :], 1.0)
        emit("gpsimd", "memset", ones_col[:, :], 1.0)
        emit("gpsimd", "memset", stcol[:, :], 0.0)
        emit("gpsimd", "memset", epsc[:, :], EPS)
        emit("gpsimd", "memset", sc32[:, :], 1.0 / 32.0)
        emit("gpsimd", "iota", partcol[:, 0:1], [[0, 1]], channel_multiplier=1,
             allow_small_or_imprecise_dtypes=True)
        g_init = CNT["pG"]
        emit("vector", "memset", st_sb[:, :], 0)
        emit("vector", "memset", BUFY[:, 0:1024], 0.0)
        ztab = mark("vector", "pV")
        wait("sync", "pV", ztab)
        DMASYNC("sync", ap(dstok, 0, [[1, 128], [128, 26]]), st_sb[:, :], "dX",
            allow_slow_non_contiguous=True)
        DMASYNC("sync", ap(deog, TRASH * D, [[D, EOGR - TRASH], [1, D]]),
            BUFY[0:EOGR - TRASH, 0:1024], "dX")
        g_tab = SYNC_C = CNT["ws0"]
        wait("scalar", "dX", CONST_DX)
        wait("scalar", "ws0", SYNC_C)
        emit("scalar", "activation", Asig[:, :], BUF0[:, 0:128], AF.Sigmoid)
        s_init = CNT["pS"]
        for e in ("tensor", "vector"):
            wait(e, "pG", g_init)
            wait(e, "pS", s_init)
        wait("vector", "dX", CONST_DX)
        wait("vector", "ws0", SYNC_C)
        emit("vector", "tensor_scalar", ecr[:, :], iota_r[:, :], float(CAP),
             None, OP.mult)
        wait("sync", "ws0", g_tab)

        XSLs = [0, 1024]
        XLTs = [2048, 3072]
        GQ, BXo, HSo = 0, 5632, 5888
        X1NT, YSIL, H2R = 2048, 3072, 5120
        X1NFs = [0, 3072]

        def ln_stats(lo_ap, hi_ap, rc=15, ncol=19, base=0):
            emit("vector", "bn_stats", cols[:, base + 0:base + 6], lo_ap)
            emit("vector", "bn_stats", cols[:, base + 6:base + 12], hi_ap)
            emit("vector", "bn_aggr", cols[:, base + 12:base + 14],
                 cols[:, base + 0:base + 12])
            v = mark("vector", "pV")
            wait("scalar", "pV", v)
            emit("scalar", "activation", cols[:, base + 14:base + 15],
                 cols[:, base + 13:base + 14],
                 AF.Sqrt, bias=epsc[:, 0:1])
            s = mark("scalar", "pS")
            wait("vector", "pS", s)
            emit("vector", "reciprocal", cols[:, rc:rc + 1],
                 cols[:, base + 14:base + 15])
            emit("vector", "tensor_scalar", cols[:, ncol:ncol + 1],
                 cols[:, base + 12:base + 13],
                 cols[:, rc:rc + 1], -1.0, OP.mult, OP.mult)
            return mark("vector", "pV")

        def normalize(dst, src, vmark_val, rc=15, ncol=19):
            wait("scalar", "pV", vmark_val)
            emit("scalar", "activation", dst, src, AF.Identity,
                 bias=cols[:, ncol:ncol + 1], scale=cols[:, rc:rc + 1])
            return mark("scalar", "pS")

        tr_i = [0]

        def transpose_to(dst_ap, src_ap):
            b = 6 + (tr_i[0] % 2)
            tr_i[0] += 1
            open_banks(b)
            emit("tensor", "transpose", bank(b, 0, 128), src_ap, ident[:, :])
            t = mark("tensor", "pT")
            wait("scalar", "pT", t)
            emit("scalar", "activation", dst_ap, bank(b, 0, 128), AF.Copy)
            sm = mark("scalar", "pS")
            set_free([b], "pS", sm)
            return sm

        out_dmas = 0
        ln_busy_v = [0, 0]
        xlt_free = [0, 0]
        x1nt_free = 0
        c_tail = [0] * 4
        c_dma = [0] * 4
        x1nt_busy_v = 0
        xs_busy = [0, 0]

        for h in range(2):
            chunks = ([("warm", 0)] if h == 0 else []) + [("own", tc) for tc in range(NTC)]
            fstate = [None] * len(chunks)
            lstate = [None] * len(chunks)

            def ln_load(ci):
                kind, tc = chunks[ci]
                par = ci % 2
                if kind == "warm":
                    src = ap(dxw, 0, [[D, 128], [1, D]])
                else:
                    row0 = h * HALF + tc * 128
                    src = ap(dx, row0 * D, [[D, 128], [1, D]])
                wait("scalar", "pV", ln_busy_v[par])
                sname = "dX" if par == 0 else "dL"
                lstate[ci] = (sname,
                              dma("scalar", BUF4[:, XSLs[par]:XSLs[par] + D],
                                  src, sname))

            def ln_front(ci):
                kind, tc = chunks[ci]
                par = ci % 2
                XSL, XLT = XSLs[par], XLTs[par]
                OB = 32 * par
                wait("vector", lstate[ci][0], lstate[ci][1])
                vm = ln_stats(BUF4[:, XSL:XSL + 512], BUF4[:, XSL + 512:XSL + D],
                              rc=OB + 15, ncol=OB + 19, base=OB)
                if kind == "own":
                    xo = tc * D
                    sm = normalize(BUF0[:, xo:xo + D], BUF4[:, XSL:XSL + D], vm,
                                   rc=OB + 15, ncol=OB + 19)
                    xn_lo, xn_hi = BUF0[:, xo:xo + 512], BUF0[:, xo + 512:xo + D]
                    xn_full = BUF0[:, xo:xo + D]
                else:
                    wait("scalar", "pT", xlt_free[par])
                    sm = normalize(BUF4[:, XLT:XLT + D], BUF4[:, XSL:XSL + D], vm,
                                   rc=OB + 15, ncol=OB + 19)
                    xn_lo, xn_hi = BUF4[:, XLT:XLT + 512], BUF4[:, XLT + 512:XLT + D]
                    xn_full = BUF4[:, XLT:XLT + D]
                if kind == "own" and "xn" in TAPS:
                    wait("scalar", "pS", sm)
                    orow = h * HALF + tc * 128
                    dma("scalar", ap(dtap["xn"], orow * D, [[D, 128], [1, D]]),
                        xn_full, "dO")
                if kind == "own":
                    wait("scalar", "pS", sm)
                    dma("scalar", ap(dxnh, tc * 128 * D, [[D, 128], [1, D]]),
                        BUF0[:, tc * D:(tc + 1) * D], "dH")
                fstate[ci] = (kind, tc, par, XSL, XLT, sm, xn_lo, xn_hi, xn_full)

            def ln_back(ci):
                kind, tc, par, XSL, XLT, sm, xn_lo, xn_hi, xn_full = fstate[ci]
                IB = 64
                wait("vector", "pS", sm)
                vm2 = ln_stats(xn_lo, xn_hi, rc=IB + 15, ncol=IB + 19, base=IB)
                sm2 = normalize(BUF4[:, XSL:XSL + D], xn_full, vm2,
                                rc=IB + 15, ncol=IB + 19)
                wait("vector", "pS", sm2)
                wait("vector", "pT", xlt_free[par])
                emit("vector", "tensor_tensor", BUF4[:, XLT:XLT + D],
                     BUF4[:, XSL:XSL + D], lnw_r[:, :], OP.mult)
                emit("vector", "tensor_tensor", BUF4[:, XLT:XLT + D],
                     BUF4[:, XLT:XLT + D], lnb_r[:, :], OP.add)
                ln_busy_v[par] = mark("vector", "pV")
                if ci + 2 < len(chunks):
                    ln_load(ci + 2)
                wait("tensor", "pV", ln_busy_v[par])
                ev = 0
                for dg in range(DG):
                    if kind == "own":
                        dst = BUF1[:, dg * HALF + tc * 128:dg * HALF + tc * 128 + 128]
                    else:
                        dst = BUF3[:, dg * HALF:dg * HALF + 128]
                    ev = transpose_to(dst, BUF4[:, XLT + dg * 128:XLT + dg * 128 + 128])
                xlt_free[par] = CNT["pT"]
                return ev

            ln_load(0)
            if len(chunks) > 1:
                ln_load(1)
            ln_front(0)
            for ci in range(len(chunks)):
                if ci + 1 < len(chunks):
                    ln_front(ci + 1)
                xln_evac = ln_back(ci)
            xnh_done = CNT["dH"]
            ln_s_all = CNT["pS"]

            scan_q = []

            def gen_scan():
                xe = xln_evac
                for dg in range(DG):
                    if h == 0:
                        for n in range(NST):
                            l = dg * NST + n

                            def warm(l=l, dg=dg, xe=xe):
                                if l == 0:
                                    wait("vector", "pS", xe)
                                emit("vector", "tensor_scalar", BUF4[:, BXo:BXo + WARM],
                                     BUF3[:, dg * HALF:dg * HALF + WARM],
                                     Bsb[:, l:l + 1], maskc[:, 0:1], OP.mult, OP.mult)
                                emit("vector", "tensor_tensor_scan", BUF4[:, HSo:HSo + WARM],
                                     ap(Asig, l, [[128, 128], [0, WARM]]),
                                     BUF4[:, BXo:BXo + WARM],
                                     stcol[:, l:l + 1], OP.mult, OP.add)
                                emit("vector", "tensor_copy", stcol[:, l:l + 1],
                                     BUF4[:, HSo + WARM - 1:HSo + WARM])
                            scan_q.append(warm)
                    def dinit(dg=dg, xe=xe):
                        if h != 0 and dg == 0:
                            wait("vector", "pS", xe)
                        emit("vector", "tensor_scalar",
                             BUF3[:, dg * HALF:(dg + 1) * HALF],
                             BUF1[:, dg * HALF:(dg + 1) * HALF],
                             Dsb[:, dg:dg + 1], None, OP.mult)
                    scan_q.append(dinit)
                    for n in range(NST):
                        l = dg * NST + n
                        for run in range(4):
                            def runf(l=l, dg=dg, run=run):
                                o = run * 256
                                emit("vector", "tensor_scalar", BUF4[:, BXo:BXo + 256],
                                     BUF1[:, dg * HALF + o:dg * HALF + o + 256],
                                     Bsb[:, l:l + 1], None, OP.mult)
                                emit("vector", "tensor_tensor_scan", BUF4[:, HSo:HSo + 256],
                                     ap(Asig, l, [[128, 128], [0, 256]]),
                                     BUF4[:, BXo:BXo + 256],
                                     (stcol[:, l:l + 1] if run == 0 else
                                      BUF4[:, HSo + 255:HSo + 256]), OP.mult, OP.add)
                                if run == 3:
                                    emit("vector", "tensor_copy", stcol[:, l:l + 1],
                                         BUF4[:, HSo + 255:HSo + 256])
                                emit("vector", "scalar_tensor_tensor",
                                     BUF3[:, dg * HALF + o:dg * HALF + o + 256],
                                     BUF4[:, HSo:HSo + 256], Csb[:, l:l + 1],
                                     BUF3[:, dg * HALF + o:dg * HALF + o + 256],
                                     OP.mult, OP.add)
                            scan_q.append(runf)

            gen_scan()
            scan_pos = [0]

            def drain_scan(k):
                while k > 0 and scan_pos[0] < len(scan_q):
                    scan_q[scan_pos[0]]()
                    scan_pos[0] += 1
                    k -= 1

            RXF = 0
            for tc in range(NTC):
                wait("tensor", "pS", ln_s_all)
                if "b0" in TAPS and h == 1 and tc == 0:
                    wait("scalar", "pS", ln_s_all)
                    dma("scalar", dtap["b0"].ap(), BUF0[:, tc * D:(tc + 1) * D], "dO")
                    out_dmas += 1
                    wait("tensor", "dO", CNT["dO"])
                for dg in range(DG):
                    rxf_ev = transpose_to(BUF4[:, RXF + dg * 128:RXF + dg * 128 + 128],
                                          BUF0[:, tc * D + dg * 128:tc * D + dg * 128 + 128])
                wait("tensor", "pS", rxf_ev)
                if "rx" in TAPS and h == 1 and tc == 0:
                    wait("scalar", "pS", rxf_ev)
                    dma("scalar", dtap["rx"].ap(), BUF4[:, RXF:RXF + D], "dO")
                    out_dmas += 1
                    wait("tensor", "dO", CNT["dO"])
                open_banks(4)
                for kc in range(DG):
                    emit("tensor", "matmul", bank(4, 0, NEXP),
                         BUF4[:, RXF + kc * 128:RXF + kc * 128 + 128],
                         wr_sb[:, kc * NEXP:(kc + 1) * NEXP],
                         start=(kc == 0), stop=(kc == DG - 1))
                tm = mark("tensor", "pT")
                wait("scalar", "pT", tm)
                emit("scalar", "activation", r8a[:, :], bank(4, 0, NEXP),
                     AF.Identity, scale=sc32[:, 0:1])
                rt_evac = mark("scalar", "pS")
                set_free([4], "pS", rt_evac)
                if "lg" in TAPS:
                    dma("scalar", ap(dtap["lg"], (h * HALF + tc * 128) * NEXP,
                                     [[NEXP, 128], [1, NEXP]]), r8a[:, :], "dO")
                    out_dmas += 1
                    wait("vector", "dO", CNT["dO"])
                wait("vector", "pS", rt_evac)
                V = lambda *a, **k: emit("vector", *a, **k)
                V("tensor_tensor", r8b[:, :], r8a[:, :], rb_r[:, :], OP.add)
                V("tensor_reduce", cols[:, 16:17], r8b[:, :], AX.X, OP.max)
                V("tensor_scalar", r8c[:, :], r8b[:, :], cols[:, 16:17], None, OP.is_equal)
                V("tensor_scalar", r8d[:, :], r8c[:, :], -1.0, 1.0, OP.mult, OP.add)
                V("tensor_tensor", r8c[:, :], r8c[:, :], iota_r[:, :], OP.mult)
                V("scalar_tensor_tensor", r8c[:, :], r8d[:, :], BIG, r8c[:, :], OP.mult, OP.add)
                V("tensor_reduce", cols[:, 17:18], r8c[:, :], AX.X, OP.min)
                V("tensor_scalar", r8c[:, :], iota_r[:, :], cols[:, 17:18], None, OP.is_equal)
                V("scalar_tensor_tensor", r8b[:, :], r8c[:, :], -BIG, r8b[:, :], OP.mult, OP.add)
                V("tensor_reduce", cols[:, 16:17], r8b[:, :], AX.X, OP.max)
                V("tensor_scalar", r8d[:, :], r8b[:, :], cols[:, 16:17], None, OP.is_equal)
                V("tensor_scalar", r8b[:, :], r8d[:, :], -1.0, 1.0, OP.mult, OP.add)
                V("tensor_tensor", r8d[:, :], r8d[:, :], iota_r[:, :], OP.mult)
                V("scalar_tensor_tensor", r8d[:, :], r8b[:, :], BIG, r8d[:, :], OP.mult, OP.add)
                V("tensor_reduce", cols[:, 17:18], r8d[:, :], AX.X, OP.min)
                V("tensor_scalar", r8d[:, :], iota_r[:, :], cols[:, 17:18], None, OP.is_equal)
                V("tensor_copy", M1b[:, tc * NEXP:(tc + 1) * NEXP], r8c[:, :])
                V("tensor_copy", M2b[:, tc * NEXP:(tc + 1) * NEXP], r8d[:, :])
                V("tensor_tensor", r8c[:, :], r8c[:, :], r8d[:, :], OP.add)
                V("tensor_scalar", r8d[:, :], r8c[:, :], -1.0, 1.0, OP.mult, OP.add)
                V("scalar_tensor_tensor", r8d[:, :], r8d[:, :], -BIG, r8a[:, :], OP.mult, OP.add)
                V("tensor_reduce", cols[:, 16:17], r8d[:, :], AX.X, OP.max, negate=True)
                vm = mark("vector", "pV")
                wait("scalar", "pV", vm)
                emit("scalar", "activation", r8b[:, :], r8a[:, :], AF.Exp,
                     bias=cols[:, 16:17])
                sm2 = mark("scalar", "pS")
                wait("vector", "pS", sm2)
                V("tensor_tensor", r8b[:, :], r8b[:, :], r8c[:, :], OP.mult)
                V("reduce_sum", cols[:, 17:18], r8b[:, :], axis=AX.X)
                V("reciprocal", cols[:, 18:19], cols[:, 17:18])
                V("tensor_scalar", w_T[:, tc * NEXP:(tc + 1) * NEXP], r8b[:, :],
                  cols[:, 18:19], None, OP.mult)
                V("tensor_tensor", r8b[:, :], w_T[:, tc * NEXP:(tc + 1) * NEXP],
                  M1b[:, tc * NEXP:(tc + 1) * NEXP], OP.mult)
                V("reduce_sum", W1c[:, tc:tc + 1], r8b[:, :], axis=AX.X)
                V("tensor_tensor", r8b[:, :], w_T[:, tc * NEXP:(tc + 1) * NEXP],
                  M2b[:, tc * NEXP:(tc + 1) * NEXP], OP.mult)
                V("reduce_sum", W2c[:, tc:tc + 1], r8b[:, :], axis=AX.X)
                mark("vector", "pV")
            w_ready = CNT["pV"]
            if "w" in TAPS:
                wait("scalar", "pV", w_ready)
                dma("scalar", ap(dtap["w"], h * HALF * NEXP,
                                 [[NEXP, 128], [128 * NEXP, NTC], [1, NEXP]]),
                    w_T[:, :], "dO")
                out_dmas += 1

            sel_tr = 0
            for tc in (range(NTC) if not DBG_SKIP_IDX else []):
                wait("vector", "pT", sel_tr)
                emit("vector", "tensor_tensor", BUF4[:, 0:NEXP],
                     M1b[:, tc * NEXP:(tc + 1) * NEXP],
                     M2b[:, tc * NEXP:(tc + 1) * NEXP], OP.add)
                svm = mark("vector", "pV")
                wait("tensor", "pV", svm)
                b6 = 6 + (tc % 2)
                open_banks(b6)
                emit("tensor", "transpose", bank(b6, 0, 128),
                     BUF4[:, 0:128], ident[:, :])
                sel_tr = ttm = mark("tensor", "pT")
                wait("scalar", "pT", ttm)
                emit("scalar", "activation", BUF2[0:8, tc * 128:(tc + 1) * 128],
                     bass.AP(PS, b6 * 512, [[4096, 8], [1, 128]]), AF.Copy)
                set_free([b6], "pS", mark("scalar", "pS"))
            selT_s = CNT["pS"]
            if DBG_SKIP_IDX:
                emit("vector", "memset", BUF2[0:8, 0:2 * HALF], 0.0)
            wait("vector", "pS", selT_s)
            emit("vector", "tensor_tensor_scan", BUF2[0:8, HALF:2 * HALF],
                 bass.AP(ones_col, 0, [[1, 8], [0, HALF]]), BUF2[0:8, 0:HALF],
                 0.0, OP.mult, OP.add)
            emit("vector", "tensor_tensor", BUF2[0:8, HALF:2 * HALF],
                 BUF2[0:8, HALF:2 * HALF], BUF2[0:8, 0:HALF], OP.subtract)
            rank_v = mark("vector", "pV")
            rank_cons = 0
            scat_prev = 0
            for tc in (range(NTC) if not DBG_SKIP_IDX else []):
                b6 = 6 + (tc % 2)
                wait("tensor", "pV", rank_v)
                open_banks(b6)
                emit("tensor", "transpose", bank(b6, 0, 128),
                     bass.AP(BUF2, HALF + tc * 128, [[8192, 128], [1, 128]]),
                     ident[:, :])
                ttm = mark("tensor", "pT")
                wait("scalar", "pT", ttm)
                wait("scalar", "pV", rank_cons)
                emit("scalar", "activation", r8a[:, :],
                     bass.AP(PS, b6 * 512, [[4096, 128], [1, 8]]), AF.Copy)
                rsm = mark("scalar", "pS")
                set_free([b6], "pS", rsm)
                wait("vector", "pS", rsm)
                emit("vector", "tensor_scalar", r8b[:, :], r8a[:, :], float(CAP),
                     None, OP.is_lt)
                emit("vector", "tensor_tensor", r8d[:, :], r8a[:, :], ecr[:, :], OP.add)
                emit("vector", "tensor_tensor", r8d[:, :], r8d[:, :], r8b[:, :], OP.mult)
                emit("vector", "tensor_scalar", r8c[:, :], r8b[:, :], -1.0, 1.0,
                     OP.mult, OP.add)
                emit("vector", "scalar_tensor_tensor", r8d[:, :], r8c[:, :],
                     float(TRASH), r8d[:, :], OP.mult, OP.add)
                emit("vector", "tensor_tensor", r8c[:, :], r8d[:, :],
                     M1b[:, tc * NEXP:(tc + 1) * NEXP], OP.mult)
                emit("vector", "reduce_sum", cols[:, 23:24], r8c[:, :], axis=AX.X)
                emit("vector", "tensor_tensor", r8c[:, :], r8d[:, :],
                     M2b[:, tc * NEXP:(tc + 1) * NEXP], OP.mult)
                emit("vector", "reduce_sum", cols[:, 24:25], r8c[:, :], axis=AX.X)
                emit("vector", "tensor_copy", P1i[:, tc:tc + 1], cols[:, 23:24])
                emit("vector", "tensor_copy", P2i[:, tc:tc + 1], cols[:, 24:25])
                emit("vector", "tensor_scalar", cols[:, 25:26], partcol[:, 0:1],
                     float(tc * 128), None, OP.add)
                wait("vector", "dG", scat_prev)
                emit("vector", "tensor_copy", tok_i[:, 0:1], cols[:, 25:26])
                rank_cons = idx_v = mark("vector", "pV")
                if "ps" in TAPS:
                    wait("scalar", "pV", idx_v)
                    dma("scalar", ap(dtap["ps"], (h * HALF + tc * 128) * 2,
                                     [[2, 128], [1, 1]]), cols[:, 23:24], "dO",
                        allow_slow_non_contiguous=True)
                    dma("scalar", ap(dtap["ps"], (h * HALF + tc * 128) * 2 + 1,
                                     [[2, 128], [1, 1]]), cols[:, 24:25], "dO",
                        allow_slow_non_contiguous=True)
                    out_dmas += 2
                wait("gpsimd", "pV", idx_v)
                for Pi in (P1i, P2i):
                    wait("gpsimd", "dG", CNT["dG"])
                    emit("gpsimd", "indirect_dma_start",
                         ap(dstok, 0, [[1, 3328], [1, 1]]),
                         IndirectOffsetOnAxis(ap=Pi[:, tc:tc + 1], axis=0),
                         tok_i[:, 0:1], None)
                    mark("gpsimd", "dG", 16)
                scat_prev = CNT["dG"]
            wait("gpsimd", "dG", CNT["dG"])
            dma("gpsimd", st_sb[:, 0:26], ap(dstok, 0, [[1, 128], [128, 26]]), "dG",
                allow_slow_non_contiguous=True)
            st_ready = CNT["dG"]
            if "st" in TAPS and h == 0:
                wait("scalar", "dG", st_ready)
                emit("scalar", "activation", BUF4[:, 1024:1050], st_sb[:, :], AF.Copy)
                stm = mark("scalar", "pS")
                dma("scalar", dtap["st"].ap(), BUF4[:, 1024:1050], "dO")
                out_dmas += 1

            ACCB = BUF2
            XGs = [0, 4096]
            LND = [2560, 3584]
            SLB2 = 2048
            STG = 4608
            if "xn" in TAPS:
                wait("scalar", "dO", CNT["dO"])
            groups = [("sh", 0), ("sh", 1)] + [("ex", e) for e in range(NEXP)]
            if DBG_SKIP_EXPERTS:
                groups = []
                emit("gpsimd", "memset", BUF2[:, :], 0.0)
                acc_sh = [mark("gpsimd", "pG")] * NTC
            land_free = [0, 0]
            xg_rel = [0, 0]
            sl_cons = 0
            stg_rel = 0
            g_prev_eot = 0
            eo_ev = 0
            acc_sh = [0] * NTC
            gb_busy = 0
            for gi, (gk, gv) in enumerate(groups):
                ntc_g = 4 if gk == "sh" else 3
                NT = 128 * ntc_g
                xs = XGs[gi % 2]
                xg_tr = 0
                for t in range(ntc_g):
                    ls = t % 2
                    wait("gpsimd", "pT", land_free[ls])
                    wait("gpsimd", "dH", xnh_done)
                    if gk == "sh":
                        dma("gpsimd", BUF4[:, LND[ls]:LND[ls] + D],
                            ap(dxnh, (gv * 512 + t * 128) * D, [[D, 128], [1, D]]), "dG")
                    else:
                        wait("gpsimd", "dG", CNT["dG"])
                        emit("gpsimd", "indirect_dma_start",
                             BUF4[:, LND[ls]:LND[ls] + D], None,
                             ap(dxnh, 0, [[D, HALF], [1, D]]),
                             IndirectOffsetOnAxis(ap=st_sb[:, gv * 3 + t:gv * 3 + t + 1], axis=0))
                        mark("gpsimd", "dG", 16)
                    gdm = CNT["dG"]
                    wait("tensor", "dG", gdm)
                    wait("tensor", "pT", xg_rel[gi % 2])
                    for dg in range(DG):
                        xg_tr = transpose_to(BUF0[:, xs + dg * 512 + t * 128:
                                                  xs + dg * 512 + t * 128 + 128],
                                             BUF4[:, LND[ls] + dg * 128:LND[ls] + dg * 128 + 128])
                    land_free[ls] = CNT["pT"]
                for hk in range(16):
                    par = hk % 2
                    if gk == "sh":
                        w1src = ap(dWs1, hk * 128, [[HE, 128], [128 * HE, DG], [1, 128]])
                        w3src = ap(dWs3, hk * 128, [[HE, 128], [128 * HE, DG], [1, 128]])
                    else:
                        base = gv * D * HE
                        w1src = ap(dW1, base + hk * 128, [[HE, 128], [128 * HE, DG], [1, 128]])
                        w3src = ap(dW3, base + hk * 128, [[HE, 128], [128 * HE, DG], [1, 128]])
                    wv1 = stream_slot(0, w1src)
                    wv3 = stream_slot(1, w3src)
                    wait("tensor", "ws0", wv1)
                    if hk == 0:
                        wait("tensor", "pS", xg_tr)
                    open_banks(par)
                    for kc in range(DG):
                        emit("tensor", "matmul", bank(par, 0, NT),
                             BUFY[:, kc * 128:kc * 128 + 128],
                             BUF0[:, xs + kc * 512:xs + kc * 512 + NT],
                             start=(kc == 0), stop=(kc == DG - 1))
                    yrel[0] = h1t = mark("tensor", "pT")
                    wait("tensor", "ws1", wv3)
                    open_banks(2 + par)
                    for kc in range(DG):
                        emit("tensor", "matmul", bank(2 + par, 0, NT),
                             BUFY[:, 1024 + kc * 128:1024 + kc * 128 + 128],
                             BUF0[:, xs + kc * 512:xs + kc * 512 + NT],
                             start=(kc == 0), stop=(kc == DG - 1))
                    yrel[1] = h3t = mark("tensor", "pT")
                    wait("scalar", "pT", h3t)
                    if sl_cons > 0:
                        wait("scalar", "pV", sl_cons)
                    emit("scalar", "activation", BUF4[:, SLB2:SLB2 + NT],
                         bank(2 + par, 0, NT), AF.Silu)
                    slm = mark("scalar", "pS")
                    set_free([2 + par], "pS", slm)
                    wait("vector", "pS", slm)
                    wait("vector", "pT", h1t)
                    if g_prev_eot > 0:
                        wait("vector", "pT", g_prev_eot)
                    emit("vector", "tensor_tensor",
                         BUF4[:, GQ + (hk % 4) * 512:GQ + (hk % 4) * 512 + NT],
                         bank(par, 0, NT), BUF4[:, SLB2:SLB2 + NT], OP.mult)
                    sl_cons = g_last = mark("vector", "pV")
                    set_free([par], "pV", g_last)
                    drain_scan(4)
                    if hk % 4 == 3:
                        q = hk // 4
                        w2w = [0] * 4
                        for i in range(4):
                            hk2 = q * 4 + i
                            if gk == "sh":
                                src = ap(dWs2, hk2 * 128 * D, [[D, 128], [1, D]])
                            else:
                                src = ap(dW2, gv * HE * D + hk2 * 128 * D, [[D, 128], [1, D]])
                            w2w[i] = stream_slot(2 + i, src)
                        for t in range(ntc_g):
                            pb = 4 + (t % 2) * 2
                            wait("tensor", "pV", g_last)
                            open_banks(pb, pb + 1)
                            for i in range(4):
                                wait("tensor", "ws%d" % (2 + i), w2w[i])
                                for dh in range(2):
                                    emit("tensor", "matmul", bank(pb + dh),
                                         BUF4[:, GQ + i * 512 + t * 128:GQ + i * 512 + t * 128 + 128],
                                         BUFY[:, (2 + i) * 1024 + dh * 512:(2 + i) * 1024 + dh * 512 + 512],
                                         start=(i == 0), stop=(i == 3))
                            g_prev_eot = eot = mark("tensor", "pT")
                            if gk == "sh":
                                tc = gv * 4 + t
                                wait("vector", "pT", eot)
                                if q == 0:
                                    emit("vector", "tensor_copy",
                                         ACCB[:, tc * D:(tc + 1) * D], bank2(pb))
                                else:
                                    emit("vector", "tensor_tensor",
                                         ACCB[:, tc * D:(tc + 1) * D], bank2(pb),
                                         ACCB[:, tc * D:(tc + 1) * D], OP.add)
                                eo_ev = mark("vector", "pV")
                                if q == 3:
                                    acc_sh[tc] = eo_ev
                                set_free([pb, pb + 1], "pV", eo_ev)
                            else:
                                wait("scalar", "pT", eot)
                                if stg_rel > 0:
                                    wait("scalar", "dE", stg_rel)
                                emit("scalar", "activation", BUF4[:, STG:STG + D],
                                     bank2(pb), AF.Copy)
                                ssm = mark("scalar", "pS")
                                set_free([pb, pb + 1], "pS", ssm)
                                row0 = gv * CAP + t * 128
                                wait("gpsimd", "pS", ssm)
                                dma("gpsimd", ap(deog, row0 * D, [[D, 128], [1, D]]),
                                    BUF4[:, STG:STG + D], "dE",
                                    **({} if q == 0 else {"accum_op": OP.add}))
                                stg_rel = CNT["dE"]
                            drain_scan(3)
                xg_rel[gi % 2] = h3t
            eog_done = CNT["dE"]
            b_eot = g_prev_eot
            drain_scan(10 ** 9)
            scan_done = CNT["pV"]

            wait("gpsimd", "dE", eog_done)
            gbsem = ["dG", "dGb", "dG2", "dG3"]
            gb_pair_busy = [0, 0]
            for tc in (range(NTC) if not DBG_SKIP_EXPERTS else []):
                pr = tc % 2
                for k, Pi in ((0, P1i), (1, P2i)):
                    ls = pr * 2 + k
                    if gb_pair_busy[pr] > 0:
                        wait("gpsimd", "pV", gb_pair_busy[pr])
                    wait("gpsimd", gbsem[ls], CNT[gbsem[ls]])
                    emit("gpsimd", "indirect_dma_start",
                         BUF0[:, ls * 1024:ls * 1024 + D], None,
                         ap(deog, 0, [[D, EOGR], [1, D]]),
                         IndirectOffsetOnAxis(ap=Pi[:, tc:tc + 1], axis=0))
                    mark("gpsimd", gbsem[ls], 16)
                wait("vector", gbsem[pr * 2], CNT[gbsem[pr * 2]])
                wait("vector", gbsem[pr * 2 + 1], CNT[gbsem[pr * 2 + 1]])
                wait("vector", "pV", acc_sh[tc])
                emit("vector", "scalar_tensor_tensor",
                     ACCB[:, tc * D:(tc + 1) * D], BUF0[:, pr * 2048:pr * 2048 + D],
                     W1c[:, tc:tc + 1], ACCB[:, tc * D:(tc + 1) * D],
                     OP.mult, OP.add)
                emit("vector", "scalar_tensor_tensor",
                     ACCB[:, tc * D:(tc + 1) * D],
                     BUF0[:, pr * 2048 + 1024:pr * 2048 + 1024 + D],
                     W2c[:, tc:tc + 1], ACCB[:, tc * D:(tc + 1) * D],
                     OP.mult, OP.add)
                gb_pair_busy[pr] = mark("vector", "pV")
            acc_done = CNT["pV"]
            if DBG_SKIP_EXPERTS:
                b_eot = CNT["pT"]
            if "acc" in TAPS:
                wait("scalar", "pV", acc_done)
                for tc in range(NTC):
                    orow = h * HALF + tc * 128
                    dma("scalar", ap(dtap["acc"], orow * D, [[D, 128], [1, D]]),
                        ACCB[:, tc * D:(tc + 1) * D], "dO")
                    out_dmas += 1
            if "y" in TAPS:
                wait("scalar", "pV", scan_done)
                for dg in range(DG):
                    dma("scalar", ap(dtap["y"], (h * HALF) * D + dg * 128,
                                     [[1, 128], [D, HALF]]),
                        BUF3[:, dg * HALF:(dg + 1) * HALF], "dO",
                        allow_slow_non_contiguous=True)
                    out_dmas += 1

            s_evac_last = 0
            prev_mmdone = 0
            for tsup in range(4):
                toff = tsup * 256
                tc0 = tsup * 2
                if tsup == 0:
                    wait("scalar", "pV", scan_done)
                    wait("scalar", "dE", eog_done)
                    for dg in range(DG):
                        emit("scalar", "activation",
                             BUF3[:, dg * HALF:(dg + 1) * HALF],
                             BUF3[:, dg * HALF:(dg + 1) * HALF], AF.Silu)
                ysil = mark("scalar", "pS")
                if "ys" in TAPS:
                    wait("scalar", "pS", ysil)
                    for dg in range(DG):
                        dma("scalar", ap(dtap["ys"], (h * HALF + toff) * D + dg * 128,
                                         [[1, 128], [D, 256]]),
                            BUF4[:, YSIL + dg * 256:YSIL + dg * 256 + 256], "dO",
                            allow_slow_non_contiguous=True)
                        out_dmas += 1
                    wait("tensor", "dO", CNT["dO"])
                bb = (tsup % 2) * 4
                rcs, ncs = [15, 21], [19, 22]
                for tcoff in range(2):
                    tc = tc0 + tcoff
                    xs = tcoff * 1024
                    wait("scalar", "pV", xs_busy[tcoff])
                    wait("scalar", "pT", b_eot)
                    row0 = h * HALF + tc * 128
                    dvx = dma("scalar", BUF4[:, xs:xs + D],
                              ap(dx, row0 * D, [[D, 128], [1, D]]), "dX")
                    wait("vector", "dX", dvx)
                    ln_stats(BUF4[:, xs:xs + 512], BUF4[:, xs + 512:xs + D],
                             rc=rcs[tcoff], ncol=ncs[tcoff])
                for k in range(DG):
                    slot = k % 2
                    wv = stream_slot(slot, ap(doutwT, k * 128 * D, [[D, 128], [1, D]]))
                    wait("tensor", "ws%d" % slot, wv)
                    if k == 0:
                        wait("tensor", "pS", ysil)
                        open_banks(bb, bb + 1, bb + 2, bb + 3)
                        for tcoff in range(2):
                            for oh in range(2):
                                if DBG_SKIP_BIAS:
                                    continue
                                emit("tensor", "matmul", bank(bb + tcoff * 2 + oh),
                                     ones_row[0:1, 0:128],
                                     outb_row[0:1, oh * 512:oh * 512 + 512],
                                     start=True, stop=False)
                    for tcoff in range(2):
                        for oh in range(2):
                            emit("tensor", "matmul", bank(bb + tcoff * 2 + oh),
                                 BUF3[:, k * HALF + toff + tcoff * 128:
                                      k * HALF + toff + tcoff * 128 + 128],
                                 BUFY[:, slot * 1024 + oh * 512:
                                      slot * 1024 + oh * 512 + 512],
                                 start=(DBG_SKIP_BIAS and k == 0),
                                 stop=(k == DG - 1))
                    yrel[slot] = mark("tensor", "pT")
                prev_mmdone = mmdone = CNT["pT"]
                x1m = [0, 0]
                for tcoff in range(2):
                    tc = tc0 + tcoff
                    xs = tcoff * 1024
                    wait("vector", "pT", mmdone)
                    if "x1" in TAPS:
                        wait("vector", "dO", CNT["dO"])
                    emit("vector", "scalar_tensor_tensor",
                         BUF0[:, tc * D:(tc + 1) * D],
                         BUF4[:, xs:xs + D], cols[:, rcs[tcoff]:rcs[tcoff] + 1],
                         bank2(bb + tcoff * 2), OP.mult, OP.add)
                    emit("vector", "tensor_scalar",
                         BUF0[:, tc * D:(tc + 1) * D],
                         BUF0[:, tc * D:(tc + 1) * D],
                         cols[:, ncs[tcoff]:ncs[tcoff] + 1], None, OP.add)
                    x1m[tcoff] = xs_busy[tcoff] = mark("vector", "pV")
                    if "x1" in TAPS:
                        wait("scalar", "pV", x1m[tcoff])
                        orow = h * HALF + tc * 128
                        dma("scalar", ap(dtap["x1"], orow * D, [[D, 128], [1, D]]),
                            BUF0[:, tc * D:(tc + 1) * D], "dO")
                        out_dmas += 1
                set_free([bb, bb + 1], "pV", x1m[0])
                set_free([bb + 2, bb + 3], "pV", x1m[1])
            x1_all = CNT["pV"]

            x1n_ready_t = [0] * 4
            gel = [0, 0]
            f2rel = [0, 0]
            sdone_t = [0] * 4
            for step in range(6):
                if step < 4:
                    tsup = step
                    tc0 = tsup * 2
                    xpar = tsup % 2
                    X1NF = X1NFs[xpar]
                    for tcoff in range(2):
                        tc = tc0 + tcoff
                        x1a = BUF0[:, tc * D:(tc + 1) * D]
                        wait("vector", "pV", x1_all)
                        vm = ln_stats(BUF0[:, tc * D:tc * D + 512],
                                      BUF0[:, tc * D + 512:(tc + 1) * D])
                        wait("scalar", "pT", x1nt_free)
                        wait("scalar", "pV", x1nt_busy_v)
                        sm = normalize(BUF4[:, X1NT:X1NT + D], x1a, vm)
                        wait("vector", "pS", sm)
                        emit("vector", "tensor_tensor", BUF4[:, X1NT:X1NT + D],
                             BUF4[:, X1NT:X1NT + D], ln2w_r[:, :], OP.mult)
                        emit("vector", "tensor_tensor", BUF4[:, X1NT:X1NT + D],
                             BUF4[:, X1NT:X1NT + D], ln2b_r[:, :], OP.add)
                        va = x1nt_busy_v = mark("vector", "pV")
                        wait("tensor", "pV", va)
                        for dg in range(DG):
                            o = X1NF + dg * 256 + tcoff * 128
                            x1n_ready_t[tsup] = transpose_to(
                                BUF4[:, o:o + 128],
                                BUF4[:, X1NT + dg * 128:X1NT + dg * 128 + 128])
                        x1nt_free = CNT["pT"]
                if 0 <= step - 2 <= 3:
                    tsup = step - 2
                    tc0 = tsup * 2
                    sdone = sdone_t[tsup]
                    for tcoff in range(2):
                        tc = tc0 + tcoff
                        wait("vector", "pT", sdone)
                        if tsup == 0 and tcoff == 0:
                            wait("vector", "pV", scan_done)
                        emit("vector", "tensor_tensor", BUF1[:, tc * D:(tc + 1) * D],
                             bank2(tcoff * 2),
                             BUF0[:, tc * D:(tc + 1) * D], OP.add)
                        s_evac_last = mark("vector", "pV")
                        if "s" in TAPS:
                            wait("scalar", "pV", s_evac_last)
                            orow = h * HALF + tc * 128
                            dma("scalar", ap(dtap["s"], orow * D, [[D, 128], [1, D]]),
                                BUF1[:, tc * D:(tc + 1) * D], "dO")
                            out_dmas += 1
                    set_free([0, 1], "pV", s_evac_last)
                    set_free([2, 3], "pV", s_evac_last)
                    for tcoff in range(2):
                        tc = tc0 + tcoff
                        sl4 = tc % 4
                        csem = "dC%d" % sl4
                        xre = sl4 * 1024
                        gg = 4096 + sl4 * 1024
                        wait("scalar", "pV", max(acc_done, c_tail[sl4], s_evac_last))
                        if c_dma[sl4] > 0:
                            wait("scalar", csem, c_dma[sl4])
                        if TAPS:
                            wait("scalar", "dO", CNT["dO"])
                            wait("vector", "dO", CNT["dO"])
                        emit("scalar", "activation", BUF3[:, gg:gg + D],
                             BUF2[:, tc * D:(tc + 1) * D], AF.Silu)
                        gsm = mark("scalar", "pS")
                        row0 = h * HALF + tc * 128
                        dv = dma("scalar", BUF3[:, xre:xre + D],
                                 ap(dx, row0 * D, [[D, 128], [1, D]]), "dX")
                        wait("vector", "pS", gsm)
                        wait("vector", "dX", dv)
                        wait("vector", "pV", s_evac_last)
                        if c_dma[sl4] > 0:
                            wait("vector", csem, c_dma[sl4])
                        emit("vector", "tensor_tensor", BUF3[:, gg:gg + D],
                             BUF1[:, tc * D:(tc + 1) * D], BUF3[:, gg:gg + D], OP.mult)
                        emit("vector", "tensor_tensor", BUF3[:, xre:xre + D],
                             BUF3[:, xre:xre + D], BUF3[:, gg:gg + D], OP.add)
                        c_tail[sl4] = mark("vector", "pV")
                        wait("scalar", "pV", c_tail[sl4])
                        c_dma[sl4] = dma("scalar", ap(dout, row0 * D, [[D, 128], [1, D]]),
                                         BUF3[:, xre:xre + D], csem)
                        out_dmas += 1
                if 1 <= step <= 4:
                    tsup = step - 1
                    tc0 = tsup * 2
                    xpar = tsup % 2
                    X1NF = X1NFs[xpar]
                    for hk in range(32):
                        par = hk % 2
                        w1 = stream_slot(2 + par,
                                         ap(dffn1T, hk * 128, [[H2, 128], [128 * H2, DG], [1, 128]]))
                        w2 = stream_slot(4 + par,
                                         ap(dffn2T, hk * 128 * D, [[D, 128], [1, D]]))
                        wait("tensor", "ws%d" % (2 + par), w1)
                        if hk == 0:
                            wait("tensor", "pS", x1n_ready_t[tsup])
                        open_banks(4 + par)
                        for kc in range(DG):
                            emit("tensor", "matmul", bank(4 + par, 0, 256),
                                 BUFY[:, (2 + par) * 1024 + kc * 128:
                                      (2 + par) * 1024 + kc * 128 + 128],
                                 BUF4[:, X1NF + kc * 256:X1NF + kc * 256 + 256],
                                 start=(kc == 0), stop=(kc == DG - 1))
                        yrel[2 + par] = f1t = mark("tensor", "pT")
                        wait("scalar", "pT", f1t)
                        if gel[par] > 0:
                            wait("scalar", "pT", f2rel[par])
                        emit("scalar", "activation",
                             BUF4[:, H2R + par * 256:H2R + par * 256 + 256],
                             bank(4 + par, 0, 256), AF.Gelu, bias=ffn1b_c[:, hk:hk + 1])
                        gel[par] = mark("scalar", "pS")
                        set_free([4 + par], "pS", gel[par])
                        wait("tensor", "ws%d" % (4 + par), w2)
                        wait("tensor", "pS", gel[par])
                        if hk == 0:
                            open_banks(0, 1, 2, 3)
                            for tcoff in range(2):
                                for dh in range(2):
                                    if DBG_SKIP_BIAS:
                                        continue
                                    emit("tensor", "matmul", bank(tcoff * 2 + dh),
                                         ones_row[0:1, 0:128],
                                         ffn2b_row[0:1, dh * 512:dh * 512 + 512],
                                         start=True, stop=False)
                        for tcoff in range(2):
                            for dh in range(2):
                                emit("tensor", "matmul", bank(tcoff * 2 + dh),
                                     BUF4[:, H2R + par * 256 + tcoff * 128:
                                          H2R + par * 256 + tcoff * 128 + 128],
                                     BUFY[:, (4 + par) * 1024 + dh * 512:
                                          (4 + par) * 1024 + dh * 512 + 512],
                                     start=(DBG_SKIP_BIAS and hk == 0),
                                     stop=(hk == 31))
                        yrel[4 + par] = f2rel[par] = mark("tensor", "pT")
                    sdone_t[tsup] = CNT["pT"]

            for cs in ("dC0", "dC1", "dC2", "dC3"):
                wait("scalar", cs, CNT[cs])
            ln_busy_v = [max(c_tail)] * 2
            xlt_free = [0, 0]

        for cs in ("dC0", "dC1", "dC2", "dC3"):
            wait("scalar", cs, CNT[cs])
        if CNT["dO"] > 0:
            wait("scalar", "dO", CNT["dO"])

        block = ES.enter_context(nc.Block())

        def mk_replay(q, eng):
            def replay(e):
                for idx, item in enumerate(q):
                    try:
                        if item[0] == "wait":
                            e.wait_ge(item[1], item[2])
                        else:
                            _, meth, args, kw, incs = item
                            inst = getattr(e, meth)(*args, **kw)
                            for s, a in incs:
                                inst.then_inc(s, a)
                    except Exception:
                        import sys as _s
                        print("REPLAY FAIL eng=%s idx=%d/%d" % (eng, idx, len(q)),
                              file=_s.stderr)
                        for j in range(max(0, idx - 6), idx + 1):
                            it = q[j]
                            print("  q[%d]: %s" % (j, it[1] if it[0] == "op" else ("wait", it[2])),
                                  file=_s.stderr)
                        raise
            return replay

        for eng in ("tensor", "vector", "scalar", "gpsimd", "sync"):
            getattr(block, eng)(mk_replay(Q[eng], eng))
    return nc


def kernel(**inputs):
    x = np.ascontiguousarray(np.asarray(inputs["L0_attn_x"], dtype=np.float32))
    B, S, _ = x.shape
    per = S // 4

    w = {
        "ln_w": np.ascontiguousarray(inputs["L0_attn_ln_w"], np.float32),
        "ln_b": np.ascontiguousarray(inputs["L0_attn_ln_b"], np.float32),
        "ln2_w": np.ascontiguousarray(inputs["L0_attn_ln2_w"], np.float32),
        "ln2_b": np.ascontiguousarray(inputs["L0_attn_ln2_b"], np.float32),
        "A_log": np.ascontiguousarray(inputs["L0_attn_A_log"], np.float32),
        "B_mat": np.ascontiguousarray(inputs["L0_attn_B_mat"], np.float32),
        "C_mat": np.ascontiguousarray(inputs["L0_attn_C_mat"], np.float32),
        "D_vec": np.ascontiguousarray(inputs["L0_attn_D_vec"], np.float32),
        "outwT": np.ascontiguousarray(np.asarray(inputs["L0_attn_out_w"], dtype=np.float32).T),
        "out_b": np.ascontiguousarray(inputs["L0_attn_out_b"], np.float32),
        "ffn1T": np.ascontiguousarray(np.asarray(inputs["L0_attn_ffn1_w"], dtype=np.float32).T),
        "ffn1_b": np.ascontiguousarray(inputs["L0_attn_ffn1_b"], np.float32),
        "ffn2T": np.ascontiguousarray(np.asarray(inputs["L0_attn_ffn2_w"], dtype=np.float32).T),
        "ffn2_b": np.ascontiguousarray(inputs["L0_attn_ffn2_b"], np.float32),
        "Wr": np.ascontiguousarray(inputs["L0_ffn_Wr"], np.float32),
        "rb": np.ascontiguousarray(inputs["L0_ffn_rb"], np.float32),
        "W1": np.ascontiguousarray(inputs["L0_ffn_W1"], np.float32),
        "W2": np.ascontiguousarray(inputs["L0_ffn_W2"], np.float32),
        "W3": np.ascontiguousarray(inputs["L0_ffn_W3"], np.float32),
        "Ws1": np.ascontiguousarray(inputs["L0_ffn_Ws1"], np.float32),
        "Ws2": np.ascontiguousarray(inputs["L0_ffn_Ws2"], np.float32),
        "Ws3": np.ascontiguousarray(inputs["L0_ffn_Ws3"], np.float32),
        "c_ident": np.eye(128, dtype=np.float32),
        "c_iota": np.arange(NEXP, dtype=np.float32),
    }
    zeros_w = np.zeros((WARM, D), np.float32)
    in_maps = []
    for c in range(8):
        b, k = c // 4, c % 4
        q0 = k * per
        m = dict(w)
        m["x"] = np.ascontiguousarray(x[b, q0:q0 + per])
        m["xw"] = (zeros_w if k == 0 else
                   np.ascontiguousarray(x[b, q0 - WARM:q0]))
        m["mask"] = (np.zeros((1, 1), np.float32) if k == 0 else
                     np.ones((1, 1), np.float32))
        in_maps.append(m)

    nc = build_program()
    res = run_bass_kernel_spmd(nc, in_maps, core_ids=list(range(8)))
    out = np.empty((B, S, D), np.float32)
    for c in range(8):
        b, k = c // 4, c % 4
        out[b, k * per:(k + 1) * per] = res.results[c]["out"]
    return out



# version 17
# speedup vs baseline: 1.0180x; 1.0180x over previous
import numpy as np
import contextlib
import concourse.bass as bass
import concourse.mybir as mybir
from concourse.bass_utils import run_bass_kernel_spmd
from concourse.bass_core_rust import IndirectOffsetOnAxis

F32 = mybir.dt.float32
I32 = mybir.dt.int32
AF = mybir.ActivationFunctionType
OP = mybir.AluOpType
AX = mybir.AxisListType

D = 1024
DG = 8
NST = 16
NEXP = 8
HE = 2048
H2 = 4096
TOWN = 2048
HALF = 1024
WARM = 128
NTC = 8
CAP = 384
TRASH = 3072
EOGR = 3136
BIG = 1e30
EPS = 1e-5

TAPS = set()
DBG_SKIP_EXPERTS = False
DBG_SKIP_IDX = False
DBG_SKIP_BIAS = False


def build_program():
    nc = bass.Bass("TRN2", target_bir_lowering=False)

    dx = nc.dram_tensor("x", [TOWN, D], F32, kind="ExternalInput")
    dxw = nc.dram_tensor("xw", [WARM, D], F32, kind="ExternalInput")
    dmask = nc.dram_tensor("mask", [1, 1], F32, kind="ExternalInput")
    dlnw = nc.dram_tensor("ln_w", [D], F32, kind="ExternalInput")
    dlnb = nc.dram_tensor("ln_b", [D], F32, kind="ExternalInput")
    dln2w = nc.dram_tensor("ln2_w", [D], F32, kind="ExternalInput")
    dln2b = nc.dram_tensor("ln2_b", [D], F32, kind="ExternalInput")
    dAlog = nc.dram_tensor("A_log", [D, NST], F32, kind="ExternalInput")
    dB = nc.dram_tensor("B_mat", [D, NST], F32, kind="ExternalInput")
    dC = nc.dram_tensor("C_mat", [D, NST], F32, kind="ExternalInput")
    dD = nc.dram_tensor("D_vec", [D], F32, kind="ExternalInput")
    doutwT = nc.dram_tensor("outwT", [D, D], F32, kind="ExternalInput")
    doutb = nc.dram_tensor("out_b", [D], F32, kind="ExternalInput")
    dffn1T = nc.dram_tensor("ffn1T", [D, H2], F32, kind="ExternalInput")
    dffn1b = nc.dram_tensor("ffn1_b", [H2], F32, kind="ExternalInput")
    dffn2T = nc.dram_tensor("ffn2T", [H2, D], F32, kind="ExternalInput")
    dffn2b = nc.dram_tensor("ffn2_b", [D], F32, kind="ExternalInput")
    dWr = nc.dram_tensor("Wr", [D, NEXP], F32, kind="ExternalInput")
    drb = nc.dram_tensor("rb", [NEXP], F32, kind="ExternalInput")
    dW1 = nc.dram_tensor("W1", [NEXP, D, HE], F32, kind="ExternalInput")
    dW2 = nc.dram_tensor("W2", [NEXP, HE, D], F32, kind="ExternalInput")
    dW3 = nc.dram_tensor("W3", [NEXP, D, HE], F32, kind="ExternalInput")
    dWs1 = nc.dram_tensor("Ws1", [D, HE], F32, kind="ExternalInput")
    dWs2 = nc.dram_tensor("Ws2", [HE, D], F32, kind="ExternalInput")
    dWs3 = nc.dram_tensor("Ws3", [D, HE], F32, kind="ExternalInput")
    dident = nc.dram_tensor("c_ident", [128, 128], F32, kind="ExternalInput")
    dxnh = nc.dram_tensor("xnh", [HALF, D], F32)
    dstok = nc.dram_tensor("srctok", [3328], I32)
    deog = nc.dram_tensor("eog", [EOGR, D], F32)
    diota = nc.dram_tensor("c_iota", [NEXP], F32, kind="ExternalInput")
    dout = nc.dram_tensor("out", [TOWN, D], F32, kind="ExternalOutput")
    dtap = {}
    for t in TAPS:
        shp = {"xn": [TOWN, D], "y": [TOWN, D], "x1": [TOWN, D],
               "s": [TOWN, D], "w": [TOWN, NEXP], "acc": [TOWN, D],
               "ys": [TOWN, D], "lg": [TOWN, NEXP], "rx": [128, D],
               "b0": [128, D], "ps": [TOWN, 2], "st": [128, 26]}[t]
        dtap[t] = nc.dram_tensor("tap_" + t, shp, F32, kind="ExternalOutput")

    with contextlib.ExitStack() as ES:
        sbuf = lambda name, shape: ES.enter_context(nc.sbuf_tensor(name, shape, F32))
        ident = sbuf("ident", [128, 128])
        lnw_r = sbuf("lnw_r", [128, D]); lnb_r = sbuf("lnb_r", [128, D])
        ln2w_r = sbuf("ln2w_r", [128, D]); ln2b_r = sbuf("ln2b_r", [128, D])
        ffn1b_c = sbuf("ffn1b_c", [128, 32])
        outb_row = sbuf("outb_row", [1, D]); ffn2b_row = sbuf("ffn2b_row", [1, D])
        ones_row = sbuf("ones_row", [1, 128])
        ones_col = sbuf("ones_col", [128, 1])
        rb_r = sbuf("rb_r", [128, NEXP]); iota_r = sbuf("iota_r", [128, NEXP])
        maskc = sbuf("maskc", [128, 1])
        epsc = sbuf("epsc", [128, 1]); sc32 = sbuf("sc32", [128, 1])
        Asig = sbuf("Asig", [128, 128])
        Bsb = sbuf("Bsb", [128, 128]); Csb = sbuf("Csb", [128, 128])
        Dsb = sbuf("Dsb", [128, DG])
        wr_sb = sbuf("wr_sb", [128, DG * NEXP])
        stcol = sbuf("stcol", [128, 128])
        w_T = sbuf("w_T", [128, NTC * NEXP])
        cols = sbuf("cols", [128, 96])
        r8a = sbuf("r8a", [128, NEXP]); r8b = sbuf("r8b", [128, NEXP])
        r8c = sbuf("r8c", [128, NEXP]); r8d = sbuf("r8d", [128, NEXP])
        M1b = sbuf("M1b", [128, NTC * NEXP]); M2b = sbuf("M2b", [128, NTC * NEXP])
        W1c = sbuf("W1c", [128, NTC]); W2c = sbuf("W2c", [128, NTC])
        P1i = ES.enter_context(nc.sbuf_tensor("P1i", [128, NTC], I32))
        P2i = ES.enter_context(nc.sbuf_tensor("P2i", [128, NTC], I32))
        st_sb = ES.enter_context(nc.sbuf_tensor("st_sb", [128, 26], I32))
        tok_i = ES.enter_context(nc.sbuf_tensor("tok_i", [128, 2], I32))
        ecr = sbuf("ecr", [128, NEXP])
        partcol = sbuf("partcol", [128, 2])
        BUF0 = sbuf("BUF0", [128, 8192])
        BUF1 = sbuf("BUF1", [128, 8192])
        BUF2 = sbuf("BUF2", [128, 8192])
        BUF3 = sbuf("BUF3", [128, 8192])
        BUF4 = sbuf("BUF4", [128, 6144])
        BUFY = sbuf("BUFY", [128, 6144])
        PS = ES.enter_context(nc.psum_tensor("PS", [128, 4096], F32))

        def bank(i, lo=0, n=512):
            return PS[:, i * 512 + lo:i * 512 + lo + n]

        def bank2(i, n=1024):
            return PS[:, i * 512:i * 512 + n]

        mk_sem = lambda name: ES.enter_context(nc.semaphore(name))
        SEM = {k: mk_sem(k) for k in
               ["pT", "pV", "pS", "pG", "dX", "dO"]
               + ["ws%d" % i for i in range(6)] + ["dC%d" % i for i in range(4)]
               + ["dH", "dG", "dE", "dGb", "dL", "dG2", "dG3", "dR0", "dR1"]}
        CNT = {k: 0 for k in SEM}

        Q = {"tensor": [], "vector": [], "scalar": [], "gpsimd": [], "sync": []}

        SELF = {"vector": "pV", "scalar": "pS", "gpsimd": "pG"}

        def emit(eng, meth, *args, **kw):
            s = SELF.get(eng)
            if s is not None and meth not in ("dma_start", "indirect_dma_start"):
                if CNT[s] > 0:
                    Q[eng].append(("wait", SEM[s], CNT[s]))
                Q[eng].append(("op", meth, args, kw, [(SEM[s], 1)]))
                CNT[s] += 1
            else:
                Q[eng].append(("op", meth, args, kw, []))

        def wait(eng, semname, val):
            if val > 0:
                Q[eng].append(("wait", SEM[semname], val))

        def mark(eng, semname, amt=1):
            s = SELF.get(eng)
            if s == semname:
                return CNT[semname]
            assert Q[eng] and Q[eng][-1][0] == "op"
            CNT[semname] += amt
            Q[eng][-1][4].append((SEM[semname], amt))
            return CNT[semname]

        def DMASYNC(eng, out_ap, in_ap, semname, **kw):
            return dma(eng, out_ap, in_ap, "ws0", **kw)

        def dma(eng, out_ap, in_ap, semname, **kw):
            if eng == "gpsimd":
                wait(eng, semname, CNT[semname])
            emit(eng, "dma_start", out_ap, in_ap, **kw)
            return mark(eng, semname, 16)

        def ap(handle, offset, pattern):
            return bass.AP(handle, offset, pattern)

        bank_free = {b: ("pV", 0) for b in range(8)}

        def open_banks(*bs):
            for b in bs:
                sem, v = bank_free[b]
                wait("tensor", sem, v)

        def set_free(bs, semname, v):
            for b in bs:
                bank_free[b] = (semname, v)

        yrel = [0] * 6

        def stream_slot(i, src_ap):
            wait("sync", "pT", yrel[i])
            return dma("sync", BUFY[:, i * 1024:i * 1024 + D], src_ap, "ws%d" % i)

        DMASYNC("sync", lnw_r[:, :], ap(dlnw, 0, [[0, 128], [1, D]]), "dX")
        DMASYNC("sync", lnb_r[:, :], ap(dlnb, 0, [[0, 128], [1, D]]), "dX")
        DMASYNC("sync", ln2w_r[:, :], ap(dln2w, 0, [[0, 128], [1, D]]), "dX")
        DMASYNC("sync", ln2b_r[:, :], ap(dln2b, 0, [[0, 128], [1, D]]), "dX")
        dma("scalar", ffn1b_c[:, :], ap(dffn1b, 0, [[1, 128], [128, 32]]), "dX",
            allow_slow_non_contiguous=True)
        dma("scalar", outb_row[:, :], ap(doutb, 0, [[0, 1], [1, D]]), "dX")
        dma("scalar", ffn2b_row[:, :], ap(dffn2b, 0, [[0, 1], [1, D]]), "dX")
        dma("scalar", rb_r[:, :], ap(drb, 0, [[0, 128], [1, NEXP]]), "dX")
        dma("scalar", iota_r[:, :], ap(diota, 0, [[0, 128], [1, NEXP]]), "dX")
        dma("scalar", maskc[:, :], ap(dmask, 0, [[0, 128], [1, 1]]), "dX")
        dma("scalar", ident[:, :], dident.ap(), "dX")
        DMASYNC("sync", BUF0[:, 0:128], ap(dAlog, 0, [[NST, 128], [NST * 128, DG], [1, NST]]), "dX")
        DMASYNC("sync", Bsb[:, :], ap(dB, 0, [[NST, 128], [NST * 128, DG], [1, NST]]), "dX")
        DMASYNC("sync", Csb[:, :], ap(dC, 0, [[NST, 128], [NST * 128, DG], [1, NST]]), "dX")
        dma("scalar", Dsb[:, :], ap(dD, 0, [[1, 128], [128, DG]]), "dX",
            allow_slow_non_contiguous=True)
        CONST_DX = dma("scalar", wr_sb[:, :], ap(dWr, 0, [[NEXP, 128], [NEXP * 128, DG], [1, NEXP]]), "dX")

        emit("gpsimd", "memset", ones_row[:, :], 1.0)
        emit("gpsimd", "memset", ones_col[:, :], 1.0)
        emit("gpsimd", "memset", stcol[:, :], 0.0)
        emit("gpsimd", "memset", epsc[:, :], EPS)
        emit("gpsimd", "memset", sc32[:, :], 1.0 / 32.0)
        emit("gpsimd", "iota", partcol[:, 0:1], [[0, 1]], channel_multiplier=1,
             allow_small_or_imprecise_dtypes=True)
        g_init = CNT["pG"]
        emit("vector", "memset", st_sb[:, :], 0)
        emit("vector", "memset", BUFY[:, 0:1024], 0.0)
        ztab = mark("vector", "pV")
        wait("sync", "pV", ztab)
        DMASYNC("sync", ap(dstok, 0, [[1, 128], [128, 26]]), st_sb[:, :], "dX",
            allow_slow_non_contiguous=True)
        DMASYNC("sync", ap(deog, TRASH * D, [[D, EOGR - TRASH], [1, D]]),
            BUFY[0:EOGR - TRASH, 0:1024], "dX")
        g_tab = SYNC_C = CNT["ws0"]
        wait("scalar", "dX", CONST_DX)
        wait("scalar", "ws0", SYNC_C)
        emit("scalar", "activation", Asig[:, :], BUF0[:, 0:128], AF.Sigmoid)
        s_init = CNT["pS"]
        for e in ("tensor", "vector"):
            wait(e, "pG", g_init)
            wait(e, "pS", s_init)
        wait("vector", "dX", CONST_DX)
        wait("vector", "ws0", SYNC_C)
        emit("vector", "tensor_scalar", ecr[:, :], iota_r[:, :], float(CAP),
             None, OP.mult)
        wait("sync", "ws0", g_tab)

        XSLs = [0, 1024]
        XLTs = [2048, 3072]
        GQ, BXo, HSo = 0, 5632, 5888
        X1NT, YSIL, H2R = 2048, 3072, 5120
        X1NFs = [0, 3072]

        def ln_stats(lo_ap, hi_ap, rc=15, ncol=19, base=0):
            emit("vector", "bn_stats", cols[:, base + 0:base + 6], lo_ap)
            emit("vector", "bn_stats", cols[:, base + 6:base + 12], hi_ap)
            emit("vector", "bn_aggr", cols[:, base + 12:base + 14],
                 cols[:, base + 0:base + 12])
            v = mark("vector", "pV")
            wait("scalar", "pV", v)
            emit("scalar", "activation", cols[:, base + 14:base + 15],
                 cols[:, base + 13:base + 14],
                 AF.Sqrt, bias=epsc[:, 0:1])
            s = mark("scalar", "pS")
            wait("vector", "pS", s)
            emit("vector", "reciprocal", cols[:, rc:rc + 1],
                 cols[:, base + 14:base + 15])
            emit("vector", "tensor_scalar", cols[:, ncol:ncol + 1],
                 cols[:, base + 12:base + 13],
                 cols[:, rc:rc + 1], -1.0, OP.mult, OP.mult)
            return mark("vector", "pV")

        def normalize(dst, src, vmark_val, rc=15, ncol=19):
            wait("scalar", "pV", vmark_val)
            emit("scalar", "activation", dst, src, AF.Identity,
                 bias=cols[:, ncol:ncol + 1], scale=cols[:, rc:rc + 1])
            return mark("scalar", "pS")

        tr_i = [0]

        def transpose_to(dst_ap, src_ap):
            b = 6 + (tr_i[0] % 2)
            tr_i[0] += 1
            open_banks(b)
            emit("tensor", "transpose", bank(b, 0, 128), src_ap, ident[:, :])
            t = mark("tensor", "pT")
            wait("scalar", "pT", t)
            emit("scalar", "activation", dst_ap, bank(b, 0, 128), AF.Copy)
            sm = mark("scalar", "pS")
            set_free([b], "pS", sm)
            return sm

        out_dmas = 0
        ln_busy_v = [0, 0]
        xlt_free = [0, 0]
        x1nt_free = 0
        c_tail = [0] * 4
        c_dma = [0] * 4
        x1nt_busy_v = 0
        xs_busy = [0, 0]

        for h in range(2):
            chunks = ([("warm", 0)] if h == 0 else []) + [("own", tc) for tc in range(NTC)]
            fstate = [None] * len(chunks)
            lstate = [None] * len(chunks)

            def ln_load(ci):
                kind, tc = chunks[ci]
                par = ci % 2
                if kind == "warm":
                    src = ap(dxw, 0, [[D, 128], [1, D]])
                else:
                    row0 = h * HALF + tc * 128
                    src = ap(dx, row0 * D, [[D, 128], [1, D]])
                wait("scalar", "pV", ln_busy_v[par])
                sname = "dX" if par == 0 else "dL"
                lstate[ci] = (sname,
                              dma("scalar", BUF4[:, XSLs[par]:XSLs[par] + D],
                                  src, sname))

            def ln_front(ci):
                kind, tc = chunks[ci]
                par = ci % 2
                XSL, XLT = XSLs[par], XLTs[par]
                OB = 32 * par
                wait("vector", lstate[ci][0], lstate[ci][1])
                vm = ln_stats(BUF4[:, XSL:XSL + 512], BUF4[:, XSL + 512:XSL + D],
                              rc=OB + 15, ncol=OB + 19, base=OB)
                if kind == "own":
                    xo = tc * D
                    sm = normalize(BUF0[:, xo:xo + D], BUF4[:, XSL:XSL + D], vm,
                                   rc=OB + 15, ncol=OB + 19)
                    xn_lo, xn_hi = BUF0[:, xo:xo + 512], BUF0[:, xo + 512:xo + D]
                    xn_full = BUF0[:, xo:xo + D]
                else:
                    wait("scalar", "pT", xlt_free[par])
                    sm = normalize(BUF4[:, XLT:XLT + D], BUF4[:, XSL:XSL + D], vm,
                                   rc=OB + 15, ncol=OB + 19)
                    xn_lo, xn_hi = BUF4[:, XLT:XLT + 512], BUF4[:, XLT + 512:XLT + D]
                    xn_full = BUF4[:, XLT:XLT + D]
                if kind == "own" and "xn" in TAPS:
                    wait("scalar", "pS", sm)
                    orow = h * HALF + tc * 128
                    dma("scalar", ap(dtap["xn"], orow * D, [[D, 128], [1, D]]),
                        xn_full, "dO")
                if kind == "own":
                    wait("scalar", "pS", sm)
                    dma("scalar", ap(dxnh, tc * 128 * D, [[D, 128], [1, D]]),
                        BUF0[:, tc * D:(tc + 1) * D], "dH")
                fstate[ci] = (kind, tc, par, XSL, XLT, sm, xn_lo, xn_hi, xn_full)

            def ln_back(ci):
                kind, tc, par, XSL, XLT, sm, xn_lo, xn_hi, xn_full = fstate[ci]
                IB = 64
                wait("vector", "pS", sm)
                vm2 = ln_stats(xn_lo, xn_hi, rc=IB + 15, ncol=IB + 19, base=IB)
                sm2 = normalize(BUF4[:, XSL:XSL + D], xn_full, vm2,
                                rc=IB + 15, ncol=IB + 19)
                wait("vector", "pS", sm2)
                wait("vector", "pT", xlt_free[par])
                emit("vector", "tensor_tensor", BUF4[:, XLT:XLT + D],
                     BUF4[:, XSL:XSL + D], lnw_r[:, :], OP.mult)
                emit("vector", "tensor_tensor", BUF4[:, XLT:XLT + D],
                     BUF4[:, XLT:XLT + D], lnb_r[:, :], OP.add)
                ln_busy_v[par] = mark("vector", "pV")
                if ci + 2 < len(chunks):
                    ln_load(ci + 2)
                wait("tensor", "pV", ln_busy_v[par])
                ev = 0
                for dg in range(DG):
                    if kind == "own":
                        dst = BUF1[:, dg * HALF + tc * 128:dg * HALF + tc * 128 + 128]
                    else:
                        dst = BUF3[:, dg * HALF:dg * HALF + 128]
                    ev = transpose_to(dst, BUF4[:, XLT + dg * 128:XLT + dg * 128 + 128])
                xlt_free[par] = CNT["pT"]
                return ev

            ln_load(0)
            if len(chunks) > 1:
                ln_load(1)
            ln_front(0)
            for ci in range(len(chunks)):
                if ci + 1 < len(chunks):
                    ln_front(ci + 1)
                xln_evac = ln_back(ci)
            xnh_done = CNT["dH"]
            ln_s_all = CNT["pS"]

            scan_q = []

            def gen_scan():
                xe = xln_evac
                for dg in range(DG):
                    if h == 0:
                        for n in range(NST):
                            l = dg * NST + n

                            def warm(l=l, dg=dg, xe=xe):
                                if l == 0:
                                    wait("vector", "pS", xe)
                                emit("vector", "tensor_scalar", BUF4[:, BXo:BXo + WARM],
                                     BUF3[:, dg * HALF:dg * HALF + WARM],
                                     Bsb[:, l:l + 1], maskc[:, 0:1], OP.mult, OP.mult)
                                emit("vector", "tensor_tensor_scan", BUF4[:, HSo:HSo + WARM],
                                     ap(Asig, l, [[128, 128], [0, WARM]]),
                                     BUF4[:, BXo:BXo + WARM],
                                     stcol[:, l:l + 1], OP.mult, OP.add)
                                emit("vector", "tensor_copy", stcol[:, l:l + 1],
                                     BUF4[:, HSo + WARM - 1:HSo + WARM])
                            scan_q.append(warm)
                    cdma_h1 = list(c_dma)

                    def dinit(dg=dg, xe=xe, cdma_h1=cdma_h1):
                        if h != 0 and dg == 0:
                            wait("vector", "pS", xe)
                            for i4 in range(4):
                                if cdma_h1[i4] > 0:
                                    wait("vector", "dC%d" % i4, cdma_h1[i4])
                        emit("vector", "tensor_scalar",
                             BUF3[:, dg * HALF:(dg + 1) * HALF],
                             BUF1[:, dg * HALF:(dg + 1) * HALF],
                             Dsb[:, dg:dg + 1], None, OP.mult)
                    scan_q.append(dinit)
                    for n in range(NST):
                        l = dg * NST + n
                        for run in range(4):
                            def runf(l=l, dg=dg, run=run):
                                o = run * 256
                                emit("vector", "tensor_scalar", BUF4[:, BXo:BXo + 256],
                                     BUF1[:, dg * HALF + o:dg * HALF + o + 256],
                                     Bsb[:, l:l + 1], None, OP.mult)
                                emit("vector", "tensor_tensor_scan", BUF4[:, HSo:HSo + 256],
                                     ap(Asig, l, [[128, 128], [0, 256]]),
                                     BUF4[:, BXo:BXo + 256],
                                     (stcol[:, l:l + 1] if run == 0 else
                                      BUF4[:, HSo + 255:HSo + 256]), OP.mult, OP.add)
                                if run == 3:
                                    emit("vector", "tensor_copy", stcol[:, l:l + 1],
                                         BUF4[:, HSo + 255:HSo + 256])
                                emit("vector", "scalar_tensor_tensor",
                                     BUF3[:, dg * HALF + o:dg * HALF + o + 256],
                                     BUF4[:, HSo:HSo + 256], Csb[:, l:l + 1],
                                     BUF3[:, dg * HALF + o:dg * HALF + o + 256],
                                     OP.mult, OP.add)
                            scan_q.append(runf)

            gen_scan()
            scan_pos = [0]

            def drain_scan(k):
                while k > 0 and scan_pos[0] < len(scan_q):
                    scan_q[scan_pos[0]]()
                    scan_pos[0] += 1
                    k -= 1

            RXF = 0
            for tc in range(NTC):
                wait("tensor", "pS", ln_s_all)
                if "b0" in TAPS and h == 1 and tc == 0:
                    wait("scalar", "pS", ln_s_all)
                    dma("scalar", dtap["b0"].ap(), BUF0[:, tc * D:(tc + 1) * D], "dO")
                    out_dmas += 1
                    wait("tensor", "dO", CNT["dO"])
                for dg in range(DG):
                    rxf_ev = transpose_to(BUF4[:, RXF + dg * 128:RXF + dg * 128 + 128],
                                          BUF0[:, tc * D + dg * 128:tc * D + dg * 128 + 128])
                wait("tensor", "pS", rxf_ev)
                if "rx" in TAPS and h == 1 and tc == 0:
                    wait("scalar", "pS", rxf_ev)
                    dma("scalar", dtap["rx"].ap(), BUF4[:, RXF:RXF + D], "dO")
                    out_dmas += 1
                    wait("tensor", "dO", CNT["dO"])
                open_banks(4)
                for kc in range(DG):
                    emit("tensor", "matmul", bank(4, 0, NEXP),
                         BUF4[:, RXF + kc * 128:RXF + kc * 128 + 128],
                         wr_sb[:, kc * NEXP:(kc + 1) * NEXP],
                         start=(kc == 0), stop=(kc == DG - 1))
                tm = mark("tensor", "pT")
                wait("scalar", "pT", tm)
                emit("scalar", "activation", r8a[:, :], bank(4, 0, NEXP),
                     AF.Identity, scale=sc32[:, 0:1])
                rt_evac = mark("scalar", "pS")
                set_free([4], "pS", rt_evac)
                if "lg" in TAPS:
                    dma("scalar", ap(dtap["lg"], (h * HALF + tc * 128) * NEXP,
                                     [[NEXP, 128], [1, NEXP]]), r8a[:, :], "dO")
                    out_dmas += 1
                    wait("vector", "dO", CNT["dO"])
                wait("vector", "pS", rt_evac)
                V = lambda *a, **k: emit("vector", *a, **k)
                V("tensor_tensor", r8b[:, :], r8a[:, :], rb_r[:, :], OP.add)
                V("tensor_reduce", cols[:, 16:17], r8b[:, :], AX.X, OP.max)
                V("tensor_scalar", r8c[:, :], r8b[:, :], cols[:, 16:17], None, OP.is_equal)
                V("tensor_scalar", r8d[:, :], r8c[:, :], -1.0, 1.0, OP.mult, OP.add)
                V("tensor_tensor", r8c[:, :], r8c[:, :], iota_r[:, :], OP.mult)
                V("scalar_tensor_tensor", r8c[:, :], r8d[:, :], BIG, r8c[:, :], OP.mult, OP.add)
                V("tensor_reduce", cols[:, 17:18], r8c[:, :], AX.X, OP.min)
                V("tensor_scalar", r8c[:, :], iota_r[:, :], cols[:, 17:18], None, OP.is_equal)
                V("scalar_tensor_tensor", r8b[:, :], r8c[:, :], -BIG, r8b[:, :], OP.mult, OP.add)
                V("tensor_reduce", cols[:, 16:17], r8b[:, :], AX.X, OP.max)
                V("tensor_scalar", r8d[:, :], r8b[:, :], cols[:, 16:17], None, OP.is_equal)
                V("tensor_scalar", r8b[:, :], r8d[:, :], -1.0, 1.0, OP.mult, OP.add)
                V("tensor_tensor", r8d[:, :], r8d[:, :], iota_r[:, :], OP.mult)
                V("scalar_tensor_tensor", r8d[:, :], r8b[:, :], BIG, r8d[:, :], OP.mult, OP.add)
                V("tensor_reduce", cols[:, 17:18], r8d[:, :], AX.X, OP.min)
                V("tensor_scalar", r8d[:, :], iota_r[:, :], cols[:, 17:18], None, OP.is_equal)
                V("tensor_copy", M1b[:, tc * NEXP:(tc + 1) * NEXP], r8c[:, :])
                V("tensor_copy", M2b[:, tc * NEXP:(tc + 1) * NEXP], r8d[:, :])
                V("tensor_tensor", r8c[:, :], r8c[:, :], r8d[:, :], OP.add)
                V("tensor_scalar", r8d[:, :], r8c[:, :], -1.0, 1.0, OP.mult, OP.add)
                V("scalar_tensor_tensor", r8d[:, :], r8d[:, :], -BIG, r8a[:, :], OP.mult, OP.add)
                V("tensor_reduce", cols[:, 16:17], r8d[:, :], AX.X, OP.max, negate=True)
                vm = mark("vector", "pV")
                wait("scalar", "pV", vm)
                emit("scalar", "activation", r8b[:, :], r8a[:, :], AF.Exp,
                     bias=cols[:, 16:17])
                sm2 = mark("scalar", "pS")
                wait("vector", "pS", sm2)
                V("tensor_tensor", r8b[:, :], r8b[:, :], r8c[:, :], OP.mult)
                V("reduce_sum", cols[:, 17:18], r8b[:, :], axis=AX.X)
                V("reciprocal", cols[:, 18:19], cols[:, 17:18])
                V("tensor_scalar", w_T[:, tc * NEXP:(tc + 1) * NEXP], r8b[:, :],
                  cols[:, 18:19], None, OP.mult)
                V("tensor_tensor", r8b[:, :], w_T[:, tc * NEXP:(tc + 1) * NEXP],
                  M1b[:, tc * NEXP:(tc + 1) * NEXP], OP.mult)
                V("reduce_sum", W1c[:, tc:tc + 1], r8b[:, :], axis=AX.X)
                V("tensor_tensor", r8b[:, :], w_T[:, tc * NEXP:(tc + 1) * NEXP],
                  M2b[:, tc * NEXP:(tc + 1) * NEXP], OP.mult)
                V("reduce_sum", W2c[:, tc:tc + 1], r8b[:, :], axis=AX.X)
                mark("vector", "pV")
            w_ready = CNT["pV"]
            if "w" in TAPS:
                wait("scalar", "pV", w_ready)
                dma("scalar", ap(dtap["w"], h * HALF * NEXP,
                                 [[NEXP, 128], [128 * NEXP, NTC], [1, NEXP]]),
                    w_T[:, :], "dO")
                out_dmas += 1

            sel_tr = 0
            for tc in (range(NTC) if not DBG_SKIP_IDX else []):
                wait("vector", "pT", sel_tr)
                emit("vector", "tensor_tensor", BUF4[:, 0:NEXP],
                     M1b[:, tc * NEXP:(tc + 1) * NEXP],
                     M2b[:, tc * NEXP:(tc + 1) * NEXP], OP.add)
                svm = mark("vector", "pV")
                wait("tensor", "pV", svm)
                b6 = 6 + (tc % 2)
                open_banks(b6)
                emit("tensor", "transpose", bank(b6, 0, 128),
                     BUF4[:, 0:128], ident[:, :])
                sel_tr = ttm = mark("tensor", "pT")
                wait("scalar", "pT", ttm)
                emit("scalar", "activation", BUF2[0:8, tc * 128:(tc + 1) * 128],
                     bass.AP(PS, b6 * 512, [[4096, 8], [1, 128]]), AF.Copy)
                set_free([b6], "pS", mark("scalar", "pS"))
            selT_s = CNT["pS"]
            if DBG_SKIP_IDX:
                emit("vector", "memset", BUF2[0:8, 0:2 * HALF], 0.0)
            wait("vector", "pS", selT_s)
            emit("vector", "tensor_tensor_scan", BUF2[0:8, HALF:2 * HALF],
                 bass.AP(ones_col, 0, [[1, 8], [0, HALF]]), BUF2[0:8, 0:HALF],
                 0.0, OP.mult, OP.add)
            emit("vector", "tensor_tensor", BUF2[0:8, HALF:2 * HALF],
                 BUF2[0:8, HALF:2 * HALF], BUF2[0:8, 0:HALF], OP.subtract)
            rank_v = mark("vector", "pV")
            rank_cons = 0
            scat_prev = 0
            for tc in (range(NTC) if not DBG_SKIP_IDX else []):
                b6 = 6 + (tc % 2)
                wait("tensor", "pV", rank_v)
                open_banks(b6)
                emit("tensor", "transpose", bank(b6, 0, 128),
                     bass.AP(BUF2, HALF + tc * 128, [[8192, 128], [1, 128]]),
                     ident[:, :])
                ttm = mark("tensor", "pT")
                wait("scalar", "pT", ttm)
                wait("scalar", "pV", rank_cons)
                emit("scalar", "activation", r8a[:, :],
                     bass.AP(PS, b6 * 512, [[4096, 128], [1, 8]]), AF.Copy)
                rsm = mark("scalar", "pS")
                set_free([b6], "pS", rsm)
                wait("vector", "pS", rsm)
                emit("vector", "tensor_scalar", r8b[:, :], r8a[:, :], float(CAP),
                     None, OP.is_lt)
                emit("vector", "tensor_tensor", r8d[:, :], r8a[:, :], ecr[:, :], OP.add)
                emit("vector", "tensor_tensor", r8d[:, :], r8d[:, :], r8b[:, :], OP.mult)
                emit("vector", "tensor_scalar", r8c[:, :], r8b[:, :], -1.0, 1.0,
                     OP.mult, OP.add)
                emit("vector", "scalar_tensor_tensor", r8d[:, :], r8c[:, :],
                     float(TRASH), r8d[:, :], OP.mult, OP.add)
                emit("vector", "tensor_tensor", r8c[:, :], r8d[:, :],
                     M1b[:, tc * NEXP:(tc + 1) * NEXP], OP.mult)
                emit("vector", "reduce_sum", cols[:, 23:24], r8c[:, :], axis=AX.X)
                emit("vector", "tensor_tensor", r8c[:, :], r8d[:, :],
                     M2b[:, tc * NEXP:(tc + 1) * NEXP], OP.mult)
                emit("vector", "reduce_sum", cols[:, 24:25], r8c[:, :], axis=AX.X)
                emit("vector", "tensor_copy", P1i[:, tc:tc + 1], cols[:, 23:24])
                emit("vector", "tensor_copy", P2i[:, tc:tc + 1], cols[:, 24:25])
                emit("vector", "tensor_scalar", cols[:, 25:26], partcol[:, 0:1],
                     float(tc * 128), None, OP.add)
                wait("vector", "dG", scat_prev)
                emit("vector", "tensor_copy", tok_i[:, 0:1], cols[:, 25:26])
                rank_cons = idx_v = mark("vector", "pV")
                if "ps" in TAPS:
                    wait("scalar", "pV", idx_v)
                    dma("scalar", ap(dtap["ps"], (h * HALF + tc * 128) * 2,
                                     [[2, 128], [1, 1]]), cols[:, 23:24], "dO",
                        allow_slow_non_contiguous=True)
                    dma("scalar", ap(dtap["ps"], (h * HALF + tc * 128) * 2 + 1,
                                     [[2, 128], [1, 1]]), cols[:, 24:25], "dO",
                        allow_slow_non_contiguous=True)
                    out_dmas += 2
                wait("gpsimd", "pV", idx_v)
                for Pi in (P1i, P2i):
                    wait("gpsimd", "dG", CNT["dG"])
                    emit("gpsimd", "indirect_dma_start",
                         ap(dstok, 0, [[1, 3328], [1, 1]]),
                         IndirectOffsetOnAxis(ap=Pi[:, tc:tc + 1], axis=0),
                         tok_i[:, 0:1], None)
                    mark("gpsimd", "dG", 16)
                scat_prev = CNT["dG"]
            wait("gpsimd", "dG", CNT["dG"])
            dma("gpsimd", st_sb[:, 0:26], ap(dstok, 0, [[1, 128], [128, 26]]), "dG",
                allow_slow_non_contiguous=True)
            st_ready = CNT["dG"]
            if "st" in TAPS and h == 0:
                wait("scalar", "dG", st_ready)
                emit("scalar", "activation", BUF4[:, 1024:1050], st_sb[:, :], AF.Copy)
                stm = mark("scalar", "pS")
                dma("scalar", dtap["st"].ap(), BUF4[:, 1024:1050], "dO")
                out_dmas += 1

            ACCB = BUF2
            XGs = [0, 4096]
            LND = [2560, 3584]
            SLB2 = 2048
            STG = 4608
            if "xn" in TAPS:
                wait("scalar", "dO", CNT["dO"])
            groups = [("sh", 0), ("sh", 1)] + [("ex", e) for e in range(NEXP)]
            if DBG_SKIP_EXPERTS:
                groups = []
                emit("gpsimd", "memset", BUF2[:, :], 0.0)
                acc_sh = [mark("gpsimd", "pG")] * NTC
            land_free = [0, 0]
            xg_rel = [0, 0]
            sl_cons = 0
            stg_rel = 0
            g_prev_eot = 0
            eo_ev = 0
            acc_sh = [0] * NTC
            gb_busy = 0
            for gi, (gk, gv) in enumerate(groups):
                ntc_g = 4 if gk == "sh" else 3
                NT = 128 * ntc_g
                xs = XGs[gi % 2]
                xg_tr = 0
                for t in range(ntc_g):
                    ls = t % 2
                    wait("gpsimd", "pT", land_free[ls])
                    wait("gpsimd", "dH", xnh_done)
                    if gk == "sh":
                        dma("gpsimd", BUF4[:, LND[ls]:LND[ls] + D],
                            ap(dxnh, (gv * 512 + t * 128) * D, [[D, 128], [1, D]]), "dG")
                    else:
                        wait("gpsimd", "dG", CNT["dG"])
                        emit("gpsimd", "indirect_dma_start",
                             BUF4[:, LND[ls]:LND[ls] + D], None,
                             ap(dxnh, 0, [[D, HALF], [1, D]]),
                             IndirectOffsetOnAxis(ap=st_sb[:, gv * 3 + t:gv * 3 + t + 1], axis=0))
                        mark("gpsimd", "dG", 16)
                    gdm = CNT["dG"]
                    wait("tensor", "dG", gdm)
                    wait("tensor", "pT", xg_rel[gi % 2])
                    for dg in range(DG):
                        xg_tr = transpose_to(BUF0[:, xs + dg * 512 + t * 128:
                                                  xs + dg * 512 + t * 128 + 128],
                                             BUF4[:, LND[ls] + dg * 128:LND[ls] + dg * 128 + 128])
                    land_free[ls] = CNT["pT"]
                for hk in range(16):
                    par = hk % 2
                    if gk == "sh":
                        w1src = ap(dWs1, hk * 128, [[HE, 128], [128 * HE, DG], [1, 128]])
                        w3src = ap(dWs3, hk * 128, [[HE, 128], [128 * HE, DG], [1, 128]])
                    else:
                        base = gv * D * HE
                        w1src = ap(dW1, base + hk * 128, [[HE, 128], [128 * HE, DG], [1, 128]])
                        w3src = ap(dW3, base + hk * 128, [[HE, 128], [128 * HE, DG], [1, 128]])
                    wv1 = stream_slot(0, w1src)
                    wv3 = stream_slot(1, w3src)
                    wait("tensor", "ws0", wv1)
                    if hk == 0:
                        wait("tensor", "pS", xg_tr)
                    open_banks(par)
                    for kc in range(DG):
                        emit("tensor", "matmul", bank(par, 0, NT),
                             BUFY[:, kc * 128:kc * 128 + 128],
                             BUF0[:, xs + kc * 512:xs + kc * 512 + NT],
                             start=(kc == 0), stop=(kc == DG - 1))
                    yrel[0] = h1t = mark("tensor", "pT")
                    wait("tensor", "ws1", wv3)
                    open_banks(2 + par)
                    for kc in range(DG):
                        emit("tensor", "matmul", bank(2 + par, 0, NT),
                             BUFY[:, 1024 + kc * 128:1024 + kc * 128 + 128],
                             BUF0[:, xs + kc * 512:xs + kc * 512 + NT],
                             start=(kc == 0), stop=(kc == DG - 1))
                    yrel[1] = h3t = mark("tensor", "pT")
                    wait("scalar", "pT", h3t)
                    if sl_cons > 0:
                        wait("scalar", "pV", sl_cons)
                    emit("scalar", "activation", BUF4[:, SLB2:SLB2 + NT],
                         bank(2 + par, 0, NT), AF.Silu)
                    slm = mark("scalar", "pS")
                    set_free([2 + par], "pS", slm)
                    wait("vector", "pS", slm)
                    wait("vector", "pT", h1t)
                    if g_prev_eot > 0:
                        wait("vector", "pT", g_prev_eot)
                    emit("vector", "tensor_tensor",
                         BUF4[:, GQ + (hk % 4) * 512:GQ + (hk % 4) * 512 + NT],
                         bank(par, 0, NT), BUF4[:, SLB2:SLB2 + NT], OP.mult)
                    sl_cons = g_last = mark("vector", "pV")
                    set_free([par], "pV", g_last)
                    drain_scan(4)
                    if hk % 4 == 3:
                        q = hk // 4
                        w2w = [0] * 4
                        for i in range(4):
                            hk2 = q * 4 + i
                            if gk == "sh":
                                src = ap(dWs2, hk2 * 128 * D, [[D, 128], [1, D]])
                            else:
                                src = ap(dW2, gv * HE * D + hk2 * 128 * D, [[D, 128], [1, D]])
                            w2w[i] = stream_slot(2 + i, src)
                        for t in range(ntc_g):
                            pb = 4 + (t % 2) * 2
                            wait("tensor", "pV", g_last)
                            open_banks(pb, pb + 1)
                            for i in range(4):
                                wait("tensor", "ws%d" % (2 + i), w2w[i])
                                for dh in range(2):
                                    emit("tensor", "matmul", bank(pb + dh),
                                         BUF4[:, GQ + i * 512 + t * 128:GQ + i * 512 + t * 128 + 128],
                                         BUFY[:, (2 + i) * 1024 + dh * 512:(2 + i) * 1024 + dh * 512 + 512],
                                         start=(i == 0), stop=(i == 3))
                            g_prev_eot = eot = mark("tensor", "pT")
                            if gk == "sh":
                                tc = gv * 4 + t
                                wait("vector", "pT", eot)
                                if q == 0:
                                    emit("vector", "tensor_copy",
                                         ACCB[:, tc * D:(tc + 1) * D], bank2(pb))
                                else:
                                    emit("vector", "tensor_tensor",
                                         ACCB[:, tc * D:(tc + 1) * D], bank2(pb),
                                         ACCB[:, tc * D:(tc + 1) * D], OP.add)
                                eo_ev = mark("vector", "pV")
                                if q == 3:
                                    acc_sh[tc] = eo_ev
                                set_free([pb, pb + 1], "pV", eo_ev)
                            else:
                                wait("scalar", "pT", eot)
                                if stg_rel > 0:
                                    wait("scalar", "dE", stg_rel)
                                emit("scalar", "activation", BUF4[:, STG:STG + D],
                                     bank2(pb), AF.Copy)
                                ssm = mark("scalar", "pS")
                                set_free([pb, pb + 1], "pS", ssm)
                                row0 = gv * CAP + t * 128
                                wait("gpsimd", "pS", ssm)
                                dma("gpsimd", ap(deog, row0 * D, [[D, 128], [1, D]]),
                                    BUF4[:, STG:STG + D], "dE",
                                    **({} if q == 0 else {"accum_op": OP.add}))
                                stg_rel = CNT["dE"]
                            drain_scan(3)
                xg_rel[gi % 2] = h3t
            eog_done = CNT["dE"]
            b_eot = g_prev_eot
            drain_scan(10 ** 9)
            scan_done = CNT["pV"]

            wait("gpsimd", "dE", eog_done)
            gbsem = ["dG", "dGb", "dG2", "dG3"]
            gb_pair_busy = [0, 0]
            for tc in (range(NTC) if not DBG_SKIP_EXPERTS else []):
                pr = tc % 2
                for k, Pi in ((0, P1i), (1, P2i)):
                    ls = pr * 2 + k
                    if gb_pair_busy[pr] > 0:
                        wait("gpsimd", "pV", gb_pair_busy[pr])
                    wait("gpsimd", gbsem[ls], CNT[gbsem[ls]])
                    emit("gpsimd", "indirect_dma_start",
                         BUF0[:, ls * 1024:ls * 1024 + D], None,
                         ap(deog, 0, [[D, EOGR], [1, D]]),
                         IndirectOffsetOnAxis(ap=Pi[:, tc:tc + 1], axis=0))
                    mark("gpsimd", gbsem[ls], 16)
                wait("vector", gbsem[pr * 2], CNT[gbsem[pr * 2]])
                wait("vector", gbsem[pr * 2 + 1], CNT[gbsem[pr * 2 + 1]])
                wait("vector", "pV", acc_sh[tc])
                emit("vector", "scalar_tensor_tensor",
                     ACCB[:, tc * D:(tc + 1) * D], BUF0[:, pr * 2048:pr * 2048 + D],
                     W1c[:, tc:tc + 1], ACCB[:, tc * D:(tc + 1) * D],
                     OP.mult, OP.add)
                emit("vector", "scalar_tensor_tensor",
                     ACCB[:, tc * D:(tc + 1) * D],
                     BUF0[:, pr * 2048 + 1024:pr * 2048 + 1024 + D],
                     W2c[:, tc:tc + 1], ACCB[:, tc * D:(tc + 1) * D],
                     OP.mult, OP.add)
                gb_pair_busy[pr] = mark("vector", "pV")
            acc_done = CNT["pV"]
            if DBG_SKIP_EXPERTS:
                b_eot = CNT["pT"]
            if "acc" in TAPS:
                wait("scalar", "pV", acc_done)
                for tc in range(NTC):
                    orow = h * HALF + tc * 128
                    dma("scalar", ap(dtap["acc"], orow * D, [[D, 128], [1, D]]),
                        ACCB[:, tc * D:(tc + 1) * D], "dO")
                    out_dmas += 1
            if "y" in TAPS:
                wait("scalar", "pV", scan_done)
                for dg in range(DG):
                    dma("scalar", ap(dtap["y"], (h * HALF) * D + dg * 128,
                                     [[1, 128], [D, HALF]]),
                        BUF3[:, dg * HALF:(dg + 1) * HALF], "dO",
                        allow_slow_non_contiguous=True)
                    out_dmas += 1

            s_evac_last = 0
            prev_mmdone = 0
            for tsup in range(4):
                toff = tsup * 256
                tc0 = tsup * 2
                if tsup == 0:
                    wait("scalar", "pV", scan_done)
                    wait("scalar", "dE", eog_done)
                    for dg in range(DG):
                        emit("scalar", "activation",
                             BUF3[:, dg * HALF:(dg + 1) * HALF],
                             BUF3[:, dg * HALF:(dg + 1) * HALF], AF.Silu)
                ysil = mark("scalar", "pS")
                if "ys" in TAPS:
                    wait("scalar", "pS", ysil)
                    for dg in range(DG):
                        dma("scalar", ap(dtap["ys"], (h * HALF + toff) * D + dg * 128,
                                         [[1, 128], [D, 256]]),
                            BUF4[:, YSIL + dg * 256:YSIL + dg * 256 + 256], "dO",
                            allow_slow_non_contiguous=True)
                        out_dmas += 1
                    wait("tensor", "dO", CNT["dO"])
                bb = (tsup % 2) * 4
                rcs, ncs = [15, 21], [19, 22]
                for tcoff in range(2):
                    tc = tc0 + tcoff
                    xs = tcoff * 1024
                    wait("scalar", "pV", xs_busy[tcoff])
                    wait("scalar", "pT", b_eot)
                    row0 = h * HALF + tc * 128
                    dvx = dma("scalar", BUF4[:, xs:xs + D],
                              ap(dx, row0 * D, [[D, 128], [1, D]]), "dX")
                    wait("vector", "dX", dvx)
                    ln_stats(BUF4[:, xs:xs + 512], BUF4[:, xs + 512:xs + D],
                             rc=rcs[tcoff], ncol=ncs[tcoff])
                for k in range(DG):
                    slot = k % 2
                    wv = stream_slot(slot, ap(doutwT, k * 128 * D, [[D, 128], [1, D]]))
                    wait("tensor", "ws%d" % slot, wv)
                    if k == 0:
                        wait("tensor", "pS", ysil)
                        open_banks(bb, bb + 1, bb + 2, bb + 3)
                        for tcoff in range(2):
                            for oh in range(2):
                                if DBG_SKIP_BIAS:
                                    continue
                                emit("tensor", "matmul", bank(bb + tcoff * 2 + oh),
                                     ones_row[0:1, 0:128],
                                     outb_row[0:1, oh * 512:oh * 512 + 512],
                                     start=True, stop=False)
                    for tcoff in range(2):
                        for oh in range(2):
                            emit("tensor", "matmul", bank(bb + tcoff * 2 + oh),
                                 BUF3[:, k * HALF + toff + tcoff * 128:
                                      k * HALF + toff + tcoff * 128 + 128],
                                 BUFY[:, slot * 1024 + oh * 512:
                                      slot * 1024 + oh * 512 + 512],
                                 start=(DBG_SKIP_BIAS and k == 0),
                                 stop=(k == DG - 1))
                    yrel[slot] = mark("tensor", "pT")
                prev_mmdone = mmdone = CNT["pT"]
                x1m = [0, 0]
                for tcoff in range(2):
                    tc = tc0 + tcoff
                    xs = tcoff * 1024
                    wait("vector", "pT", mmdone)
                    if "x1" in TAPS:
                        wait("vector", "dO", CNT["dO"])
                    emit("vector", "scalar_tensor_tensor",
                         BUF0[:, tc * D:(tc + 1) * D],
                         BUF4[:, xs:xs + D], cols[:, rcs[tcoff]:rcs[tcoff] + 1],
                         bank2(bb + tcoff * 2), OP.mult, OP.add)
                    emit("vector", "tensor_scalar",
                         BUF0[:, tc * D:(tc + 1) * D],
                         BUF0[:, tc * D:(tc + 1) * D],
                         cols[:, ncs[tcoff]:ncs[tcoff] + 1], None, OP.add)
                    x1m[tcoff] = xs_busy[tcoff] = mark("vector", "pV")
                    if "x1" in TAPS:
                        wait("scalar", "pV", x1m[tcoff])
                        orow = h * HALF + tc * 128
                        dma("scalar", ap(dtap["x1"], orow * D, [[D, 128], [1, D]]),
                            BUF0[:, tc * D:(tc + 1) * D], "dO")
                        out_dmas += 1
                set_free([bb, bb + 1], "pV", x1m[0])
                set_free([bb + 2, bb + 3], "pV", x1m[1])
            x1_all = CNT["pV"]

            x1n_ready_t = [0] * 4
            gel = [0, 0]
            f2rel = [0, 0]
            sdone_t = [0] * 4
            for step in range(6):
                if step < 4:
                    tsup = step
                    tc0 = tsup * 2
                    xpar = tsup % 2
                    X1NF = X1NFs[xpar]
                    for tcoff in range(2):
                        tc = tc0 + tcoff
                        x1a = BUF0[:, tc * D:(tc + 1) * D]
                        wait("vector", "pV", x1_all)
                        vm = ln_stats(BUF0[:, tc * D:tc * D + 512],
                                      BUF0[:, tc * D + 512:(tc + 1) * D])
                        wait("scalar", "pT", x1nt_free)
                        wait("scalar", "pV", x1nt_busy_v)
                        sm = normalize(BUF4[:, X1NT:X1NT + D], x1a, vm)
                        wait("vector", "pS", sm)
                        emit("vector", "tensor_tensor", BUF4[:, X1NT:X1NT + D],
                             BUF4[:, X1NT:X1NT + D], ln2w_r[:, :], OP.mult)
                        emit("vector", "tensor_tensor", BUF4[:, X1NT:X1NT + D],
                             BUF4[:, X1NT:X1NT + D], ln2b_r[:, :], OP.add)
                        va = x1nt_busy_v = mark("vector", "pV")
                        wait("tensor", "pV", va)
                        for dg in range(DG):
                            o = X1NF + dg * 256 + tcoff * 128
                            x1n_ready_t[tsup] = transpose_to(
                                BUF4[:, o:o + 128],
                                BUF4[:, X1NT + dg * 128:X1NT + dg * 128 + 128])
                        x1nt_free = CNT["pT"]
                if 0 <= step - 2 <= 3:
                    tsup = step - 2
                    tc0 = tsup * 2
                    sdone = sdone_t[tsup]
                    for tcoff in range(2):
                        tc = tc0 + tcoff
                        wait("vector", "pT", sdone)
                        if tsup == 0 and tcoff == 0:
                            wait("vector", "pV", scan_done)
                        emit("vector", "tensor_tensor", BUF1[:, tc * D:(tc + 1) * D],
                             bank2(tcoff * 2),
                             BUF0[:, tc * D:(tc + 1) * D], OP.add)
                        s_evac_last = mark("vector", "pV")
                        if "s" in TAPS:
                            wait("scalar", "pV", s_evac_last)
                            orow = h * HALF + tc * 128
                            dma("scalar", ap(dtap["s"], orow * D, [[D, 128], [1, D]]),
                                BUF1[:, tc * D:(tc + 1) * D], "dO")
                            out_dmas += 1
                    set_free([0, 1], "pV", s_evac_last)
                    set_free([2, 3], "pV", s_evac_last)
                    for tcoff in range(2):
                        tc = tc0 + tcoff
                        sl4 = tc % 4
                        csem = "dC%d" % sl4
                        xre = sl4 * 1024
                        gg = 4096 + sl4 * 1024
                        wait("scalar", "pV", max(acc_done, c_tail[sl4], s_evac_last))
                        if c_dma[sl4] > 0:
                            wait("scalar", csem, c_dma[sl4])
                        if TAPS:
                            wait("scalar", "dO", CNT["dO"])
                            wait("vector", "dO", CNT["dO"])
                        emit("scalar", "activation", BUF3[:, gg:gg + D],
                             BUF2[:, tc * D:(tc + 1) * D], AF.Silu)
                        gsm = mark("scalar", "pS")
                        row0 = h * HALF + tc * 128
                        rsem = "dR%d" % tcoff
                        dv = dma("scalar", BUF3[:, xre:xre + D],
                                 ap(dx, row0 * D, [[D, 128], [1, D]]), rsem)
                        wait("vector", "pS", gsm)
                        wait("vector", rsem, dv)
                        wait("vector", "pV", s_evac_last)
                        if c_dma[sl4] > 0:
                            wait("vector", csem, c_dma[sl4])
                        emit("vector", "tensor_tensor", BUF3[:, gg:gg + D],
                             BUF1[:, tc * D:(tc + 1) * D], BUF3[:, gg:gg + D], OP.mult)
                        emit("vector", "tensor_tensor", BUF3[:, xre:xre + D],
                             BUF3[:, xre:xre + D], BUF3[:, gg:gg + D], OP.add)
                        c_tail[sl4] = mark("vector", "pV")
                        wait("gpsimd", "pV", c_tail[sl4])
                        c_dma[sl4] = dma("gpsimd", ap(dout, row0 * D, [[D, 128], [1, D]]),
                                         BUF3[:, xre:xre + D], csem)
                        out_dmas += 1
                if 1 <= step <= 4:
                    tsup = step - 1
                    tc0 = tsup * 2
                    xpar = tsup % 2
                    X1NF = X1NFs[xpar]
                    for hk in range(32):
                        par = hk % 2
                        w1 = stream_slot(2 + par,
                                         ap(dffn1T, hk * 128, [[H2, 128], [128 * H2, DG], [1, 128]]))
                        w2 = stream_slot(4 + par,
                                         ap(dffn2T, hk * 128 * D, [[D, 128], [1, D]]))
                        wait("tensor", "ws%d" % (2 + par), w1)
                        if hk == 0:
                            wait("tensor", "pS", x1n_ready_t[tsup])
                        open_banks(4 + par)
                        for kc in range(DG):
                            emit("tensor", "matmul", bank(4 + par, 0, 256),
                                 BUFY[:, (2 + par) * 1024 + kc * 128:
                                      (2 + par) * 1024 + kc * 128 + 128],
                                 BUF4[:, X1NF + kc * 256:X1NF + kc * 256 + 256],
                                 start=(kc == 0), stop=(kc == DG - 1))
                        yrel[2 + par] = f1t = mark("tensor", "pT")
                        if tsup == 3:
                            a2_f1_last = f1t
                        wait("scalar", "pT", f1t)
                        if gel[par] > 0:
                            wait("scalar", "pT", f2rel[par])
                        emit("scalar", "activation",
                             BUF4[:, H2R + par * 256:H2R + par * 256 + 256],
                             bank(4 + par, 0, 256), AF.Gelu, bias=ffn1b_c[:, hk:hk + 1])
                        gel[par] = mark("scalar", "pS")
                        set_free([4 + par], "pS", gel[par])
                        wait("tensor", "ws%d" % (4 + par), w2)
                        wait("tensor", "pS", gel[par])
                        if hk == 0:
                            open_banks(0, 1, 2, 3)
                            for tcoff in range(2):
                                for dh in range(2):
                                    if DBG_SKIP_BIAS:
                                        continue
                                    emit("tensor", "matmul", bank(tcoff * 2 + dh),
                                         ones_row[0:1, 0:128],
                                         ffn2b_row[0:1, dh * 512:dh * 512 + 512],
                                         start=True, stop=False)
                        for tcoff in range(2):
                            for dh in range(2):
                                emit("tensor", "matmul", bank(tcoff * 2 + dh),
                                     BUF4[:, H2R + par * 256 + tcoff * 128:
                                          H2R + par * 256 + tcoff * 128 + 128],
                                     BUFY[:, (4 + par) * 1024 + dh * 512:
                                          (4 + par) * 1024 + dh * 512 + 512],
                                     start=(DBG_SKIP_BIAS and hk == 0),
                                     stop=(hk == 31))
                        yrel[4 + par] = f2rel[par] = mark("tensor", "pT")
                    sdone_t[tsup] = CNT["pT"]

            ln_busy_v = [xs_busy[0], xs_busy[1]]
            xlt_free = [x1nt_free, a2_f1_last]

        for cs in ("dC0", "dC1", "dC2", "dC3"):
            wait("scalar", cs, CNT[cs])
        if CNT["dO"] > 0:
            wait("scalar", "dO", CNT["dO"])

        block = ES.enter_context(nc.Block())

        def mk_replay(q, eng):
            def replay(e):
                for idx, item in enumerate(q):
                    try:
                        if item[0] == "wait":
                            e.wait_ge(item[1], item[2])
                        else:
                            _, meth, args, kw, incs = item
                            inst = getattr(e, meth)(*args, **kw)
                            for s, a in incs:
                                inst.then_inc(s, a)
                    except Exception:
                        import sys as _s
                        print("REPLAY FAIL eng=%s idx=%d/%d" % (eng, idx, len(q)),
                              file=_s.stderr)
                        for j in range(max(0, idx - 6), idx + 1):
                            it = q[j]
                            print("  q[%d]: %s" % (j, it[1] if it[0] == "op" else ("wait", it[2])),
                                  file=_s.stderr)
                        raise
            return replay

        for eng in ("tensor", "vector", "scalar", "gpsimd", "sync"):
            getattr(block, eng)(mk_replay(Q[eng], eng))
    return nc


def kernel(**inputs):
    x = np.ascontiguousarray(np.asarray(inputs["L0_attn_x"], dtype=np.float32))
    B, S, _ = x.shape
    per = S // 4

    w = {
        "ln_w": np.ascontiguousarray(inputs["L0_attn_ln_w"], np.float32),
        "ln_b": np.ascontiguousarray(inputs["L0_attn_ln_b"], np.float32),
        "ln2_w": np.ascontiguousarray(inputs["L0_attn_ln2_w"], np.float32),
        "ln2_b": np.ascontiguousarray(inputs["L0_attn_ln2_b"], np.float32),
        "A_log": np.ascontiguousarray(inputs["L0_attn_A_log"], np.float32),
        "B_mat": np.ascontiguousarray(inputs["L0_attn_B_mat"], np.float32),
        "C_mat": np.ascontiguousarray(inputs["L0_attn_C_mat"], np.float32),
        "D_vec": np.ascontiguousarray(inputs["L0_attn_D_vec"], np.float32),
        "outwT": np.ascontiguousarray(np.asarray(inputs["L0_attn_out_w"], dtype=np.float32).T),
        "out_b": np.ascontiguousarray(inputs["L0_attn_out_b"], np.float32),
        "ffn1T": np.ascontiguousarray(np.asarray(inputs["L0_attn_ffn1_w"], dtype=np.float32).T),
        "ffn1_b": np.ascontiguousarray(inputs["L0_attn_ffn1_b"], np.float32),
        "ffn2T": np.ascontiguousarray(np.asarray(inputs["L0_attn_ffn2_w"], dtype=np.float32).T),
        "ffn2_b": np.ascontiguousarray(inputs["L0_attn_ffn2_b"], np.float32),
        "Wr": np.ascontiguousarray(inputs["L0_ffn_Wr"], np.float32),
        "rb": np.ascontiguousarray(inputs["L0_ffn_rb"], np.float32),
        "W1": np.ascontiguousarray(inputs["L0_ffn_W1"], np.float32),
        "W2": np.ascontiguousarray(inputs["L0_ffn_W2"], np.float32),
        "W3": np.ascontiguousarray(inputs["L0_ffn_W3"], np.float32),
        "Ws1": np.ascontiguousarray(inputs["L0_ffn_Ws1"], np.float32),
        "Ws2": np.ascontiguousarray(inputs["L0_ffn_Ws2"], np.float32),
        "Ws3": np.ascontiguousarray(inputs["L0_ffn_Ws3"], np.float32),
        "c_ident": np.eye(128, dtype=np.float32),
        "c_iota": np.arange(NEXP, dtype=np.float32),
    }
    zeros_w = np.zeros((WARM, D), np.float32)
    in_maps = []
    for c in range(8):
        b, k = c // 4, c % 4
        q0 = k * per
        m = dict(w)
        m["x"] = np.ascontiguousarray(x[b, q0:q0 + per])
        m["xw"] = (zeros_w if k == 0 else
                   np.ascontiguousarray(x[b, q0 - WARM:q0]))
        m["mask"] = (np.zeros((1, 1), np.float32) if k == 0 else
                     np.ones((1, 1), np.float32))
        in_maps.append(m)

    nc = build_program()
    res = run_bass_kernel_spmd(nc, in_maps, core_ids=list(range(8)))
    out = np.empty((B, S, D), np.float32)
    for c in range(8):
        b, k = c // 4, c % 4
        out[b, k * per:(k + 1) * per] = res.results[c]["out"]
    return out



# version 18
# speedup vs baseline: 1.1208x; 1.1010x over previous
import numpy as np
import contextlib
import concourse.bass as bass
import concourse.mybir as mybir
from concourse.bass_utils import run_bass_kernel_spmd
from concourse.bass_core_rust import IndirectOffsetOnAxis

F32 = mybir.dt.float32
I32 = mybir.dt.int32
AF = mybir.ActivationFunctionType
OP = mybir.AluOpType
AX = mybir.AxisListType

D = 1024
DG = 8
NST = 16
NEXP = 8
HE = 2048
H2 = 4096
TOWN = 2048
HALF = 1024
WARM = 128
NTC = 8
CAP = 384
TRASH = 3072
EOGR = 3136
BIG = 1e30
EPS = 1e-5

TAPS = set()
DBG_SKIP_EXPERTS = False
DBG_SKIP_IDX = False
DBG_SKIP_BIAS = False


def build_program():
    nc = bass.Bass("TRN2", target_bir_lowering=False)

    dx = nc.dram_tensor("x", [TOWN, D], F32, kind="ExternalInput")
    dxw = nc.dram_tensor("xw", [WARM, D], F32, kind="ExternalInput")
    dmask = nc.dram_tensor("mask", [1, 1], F32, kind="ExternalInput")
    dlnw = nc.dram_tensor("ln_w", [D], F32, kind="ExternalInput")
    dlnb = nc.dram_tensor("ln_b", [D], F32, kind="ExternalInput")
    dln2w = nc.dram_tensor("ln2_w", [D], F32, kind="ExternalInput")
    dln2b = nc.dram_tensor("ln2_b", [D], F32, kind="ExternalInput")
    dAlog = nc.dram_tensor("A_log", [D, NST], F32, kind="ExternalInput")
    dB = nc.dram_tensor("B_mat", [D, NST], F32, kind="ExternalInput")
    dC = nc.dram_tensor("C_mat", [D, NST], F32, kind="ExternalInput")
    dD = nc.dram_tensor("D_vec", [D], F32, kind="ExternalInput")
    doutwT = nc.dram_tensor("outwT", [D, D], F32, kind="ExternalInput")
    doutb = nc.dram_tensor("out_b", [D], F32, kind="ExternalInput")
    dffn1T = nc.dram_tensor("ffn1T", [D, H2], F32, kind="ExternalInput")
    dffn1b = nc.dram_tensor("ffn1_b", [H2], F32, kind="ExternalInput")
    dffn2T = nc.dram_tensor("ffn2T", [H2, D], F32, kind="ExternalInput")
    dffn2b = nc.dram_tensor("ffn2_b", [D], F32, kind="ExternalInput")
    dWr = nc.dram_tensor("Wr", [D, NEXP], F32, kind="ExternalInput")
    drb = nc.dram_tensor("rb", [NEXP], F32, kind="ExternalInput")
    dW1 = nc.dram_tensor("W1", [NEXP, D, HE], F32, kind="ExternalInput")
    dW2 = nc.dram_tensor("W2", [NEXP, HE, D], F32, kind="ExternalInput")
    dW3 = nc.dram_tensor("W3", [NEXP, D, HE], F32, kind="ExternalInput")
    dWs1 = nc.dram_tensor("Ws1", [D, HE], F32, kind="ExternalInput")
    dWs2 = nc.dram_tensor("Ws2", [HE, D], F32, kind="ExternalInput")
    dWs3 = nc.dram_tensor("Ws3", [D, HE], F32, kind="ExternalInput")
    dident = nc.dram_tensor("c_ident", [128, 128], F32, kind="ExternalInput")
    dxnh = nc.dram_tensor("xnh", [HALF, D], F32)
    dstok = nc.dram_tensor("srctok", [3328], I32)
    deog = nc.dram_tensor("eog", [EOGR, D], F32)
    diota = nc.dram_tensor("c_iota", [NEXP], F32, kind="ExternalInput")
    dout = nc.dram_tensor("out", [TOWN, D], F32, kind="ExternalOutput")
    dtap = {}
    for t in TAPS:
        shp = {"xn": [TOWN, D], "y": [TOWN, D], "x1": [TOWN, D],
               "s": [TOWN, D], "w": [TOWN, NEXP], "acc": [TOWN, D],
               "ys": [TOWN, D], "lg": [TOWN, NEXP], "rx": [128, D],
               "b0": [128, D], "ps": [TOWN, 2], "st": [128, 26]}[t]
        dtap[t] = nc.dram_tensor("tap_" + t, shp, F32, kind="ExternalOutput")

    with contextlib.ExitStack() as ES:
        sbuf = lambda name, shape: ES.enter_context(nc.sbuf_tensor(name, shape, F32))
        ident = sbuf("ident", [128, 128])
        lnw_r = sbuf("lnw_r", [128, D]); lnb_r = sbuf("lnb_r", [128, D])
        ln2w_r = sbuf("ln2w_r", [128, D]); ln2b_r = sbuf("ln2b_r", [128, D])
        ffn1b_c = sbuf("ffn1b_c", [128, 32])
        outb_row = sbuf("outb_row", [1, D]); ffn2b_row = sbuf("ffn2b_row", [1, D])
        ones_row = sbuf("ones_row", [1, 128])
        ones_col = sbuf("ones_col", [128, 1])
        rb_r = sbuf("rb_r", [128, NEXP]); iota_r = sbuf("iota_r", [128, NEXP])
        maskc = sbuf("maskc", [128, 1])
        epsc = sbuf("epsc", [128, 1]); sc32 = sbuf("sc32", [128, 1])
        Asig = sbuf("Asig", [128, 128])
        Bsb = sbuf("Bsb", [128, 128]); Csb = sbuf("Csb", [128, 128])
        Dsb = sbuf("Dsb", [128, DG])
        wr_sb = sbuf("wr_sb", [128, DG * NEXP])
        stcol = sbuf("stcol", [128, 128])
        w_T = sbuf("w_T", [128, NTC * NEXP])
        cols = sbuf("cols", [128, 96])
        r8a = sbuf("r8a", [128, NEXP]); r8b = sbuf("r8b", [128, NEXP])
        r8c = sbuf("r8c", [128, NEXP]); r8d = sbuf("r8d", [128, NEXP])
        M1b = sbuf("M1b", [128, NTC * NEXP]); M2b = sbuf("M2b", [128, NTC * NEXP])
        W1c = sbuf("W1c", [128, NTC]); W2c = sbuf("W2c", [128, NTC])
        P1i = ES.enter_context(nc.sbuf_tensor("P1i", [128, NTC], I32))
        P2i = ES.enter_context(nc.sbuf_tensor("P2i", [128, NTC], I32))
        st_sb = ES.enter_context(nc.sbuf_tensor("st_sb", [128, 26], I32))
        tok_i = ES.enter_context(nc.sbuf_tensor("tok_i", [128, 2], I32))
        ecr = sbuf("ecr", [128, NEXP])
        partcol = sbuf("partcol", [128, 2])
        BUF0 = sbuf("BUF0", [128, 8192])
        BUF1 = sbuf("BUF1", [128, 8192])
        BUF2 = sbuf("BUF2", [128, 8192])
        BUF3 = sbuf("BUF3", [128, 8192])
        BUF4 = sbuf("BUF4", [128, 6144])
        BUFY = sbuf("BUFY", [128, 6144])
        PS = ES.enter_context(nc.psum_tensor("PS", [128, 4096], F32))

        def bank(i, lo=0, n=512):
            return PS[:, i * 512 + lo:i * 512 + lo + n]

        def bank2(i, n=1024):
            return PS[:, i * 512:i * 512 + n]

        mk_sem = lambda name: ES.enter_context(nc.semaphore(name))
        SEM = {k: mk_sem(k) for k in
               ["pT", "pV", "pS", "pG", "dX", "dO"]
               + ["ws%d" % i for i in range(6)] + ["dC%d" % i for i in range(4)]
               + ["dH", "dG", "dE", "dGb", "dL", "dG2", "dG3", "dR0", "dR1"]}
        CNT = {k: 0 for k in SEM}

        Q = {"tensor": [], "vector": [], "scalar": [], "gpsimd": [], "sync": []}

        SELF = {"vector": "pV", "scalar": "pS", "gpsimd": "pG"}

        def emit(eng, meth, *args, **kw):
            s = SELF.get(eng)
            if s is not None and meth not in ("dma_start", "indirect_dma_start"):
                if CNT[s] > 0:
                    Q[eng].append(("wait", SEM[s], CNT[s]))
                Q[eng].append(("op", meth, args, kw, [(SEM[s], 1)]))
                CNT[s] += 1
            else:
                Q[eng].append(("op", meth, args, kw, []))

        def wait(eng, semname, val):
            if val > 0:
                Q[eng].append(("wait", SEM[semname], val))

        def mark(eng, semname, amt=1):
            s = SELF.get(eng)
            if s == semname:
                return CNT[semname]
            assert Q[eng] and Q[eng][-1][0] == "op"
            CNT[semname] += amt
            Q[eng][-1][4].append((SEM[semname], amt))
            return CNT[semname]

        def DMASYNC(eng, out_ap, in_ap, semname, **kw):
            return dma(eng, out_ap, in_ap, "ws0", **kw)

        def dma(eng, out_ap, in_ap, semname, **kw):
            if eng == "gpsimd":
                wait(eng, semname, CNT[semname])
            emit(eng, "dma_start", out_ap, in_ap, **kw)
            return mark(eng, semname, 16)

        def ap(handle, offset, pattern):
            return bass.AP(handle, offset, pattern)

        bank_free = {b: ("pV", 0) for b in range(8)}

        def open_banks(*bs):
            for b in bs:
                sem, v = bank_free[b]
                wait("tensor", sem, v)

        def set_free(bs, semname, v):
            for b in bs:
                bank_free[b] = (semname, v)

        yrel = [0] * 6

        def stream_slot(i, src_ap):
            wait("sync", "pT", yrel[i])
            return dma("sync", BUFY[:, i * 1024:i * 1024 + D], src_ap, "ws%d" % i)

        DMASYNC("sync", lnw_r[:, :], ap(dlnw, 0, [[0, 128], [1, D]]), "dX")
        DMASYNC("sync", lnb_r[:, :], ap(dlnb, 0, [[0, 128], [1, D]]), "dX")
        DMASYNC("sync", ln2w_r[:, :], ap(dln2w, 0, [[0, 128], [1, D]]), "dX")
        DMASYNC("sync", ln2b_r[:, :], ap(dln2b, 0, [[0, 128], [1, D]]), "dX")
        dma("scalar", ffn1b_c[:, :], ap(dffn1b, 0, [[1, 128], [128, 32]]), "dX",
            allow_slow_non_contiguous=True)
        dma("scalar", outb_row[:, :], ap(doutb, 0, [[0, 1], [1, D]]), "dX")
        dma("scalar", ffn2b_row[:, :], ap(dffn2b, 0, [[0, 1], [1, D]]), "dX")
        dma("scalar", rb_r[:, :], ap(drb, 0, [[0, 128], [1, NEXP]]), "dX")
        dma("scalar", iota_r[:, :], ap(diota, 0, [[0, 128], [1, NEXP]]), "dX")
        dma("scalar", maskc[:, :], ap(dmask, 0, [[0, 128], [1, 1]]), "dX")
        dma("scalar", ident[:, :], dident.ap(), "dX")
        DMASYNC("sync", BUF0[:, 0:128], ap(dAlog, 0, [[NST, 128], [NST * 128, DG], [1, NST]]), "dX")
        DMASYNC("sync", Bsb[:, :], ap(dB, 0, [[NST, 128], [NST * 128, DG], [1, NST]]), "dX")
        DMASYNC("sync", Csb[:, :], ap(dC, 0, [[NST, 128], [NST * 128, DG], [1, NST]]), "dX")
        dma("scalar", Dsb[:, :], ap(dD, 0, [[1, 128], [128, DG]]), "dX",
            allow_slow_non_contiguous=True)
        CONST_DX = dma("scalar", wr_sb[:, :], ap(dWr, 0, [[NEXP, 128], [NEXP * 128, DG], [1, NEXP]]), "dX")

        emit("gpsimd", "memset", ones_row[:, :], 1.0)
        emit("gpsimd", "memset", ones_col[:, :], 1.0)
        emit("gpsimd", "memset", stcol[:, :], 0.0)
        emit("gpsimd", "memset", epsc[:, :], EPS)
        emit("gpsimd", "memset", sc32[:, :], 1.0 / 32.0)
        emit("gpsimd", "iota", partcol[:, 0:1], [[0, 1]], channel_multiplier=1,
             allow_small_or_imprecise_dtypes=True)
        g_init = CNT["pG"]
        emit("vector", "memset", st_sb[:, :], 0)
        emit("vector", "memset", BUFY[:, 0:1024], 0.0)
        ztab = mark("vector", "pV")
        wait("sync", "pV", ztab)
        DMASYNC("sync", ap(dstok, 0, [[1, 128], [128, 26]]), st_sb[:, :], "dX",
            allow_slow_non_contiguous=True)
        DMASYNC("sync", ap(deog, TRASH * D, [[D, EOGR - TRASH], [1, D]]),
            BUFY[0:EOGR - TRASH, 0:1024], "dX")
        g_tab = SYNC_C = CNT["ws0"]
        wait("scalar", "dX", CONST_DX)
        wait("scalar", "ws0", SYNC_C)
        emit("scalar", "activation", Asig[:, :], BUF0[:, 0:128], AF.Sigmoid)
        s_init = CNT["pS"]
        for e in ("tensor", "vector"):
            wait(e, "pG", g_init)
            wait(e, "pS", s_init)
        wait("vector", "dX", CONST_DX)
        wait("vector", "ws0", SYNC_C)
        emit("vector", "tensor_scalar", ecr[:, :], iota_r[:, :], float(CAP),
             None, OP.mult)
        wait("sync", "ws0", g_tab)

        XSLs = [0, 1024]
        XLTs = [2048, 3072]
        GQ, BXo, HSo = 0, 5632, 5888
        X1NT, YSIL, H2R = 2048, 3072, 5120
        X1NFs = [0, 3072]

        def ln_stats(lo_ap, hi_ap, rc=15, ncol=19, base=0):
            emit("vector", "bn_stats", cols[:, base + 0:base + 6], lo_ap)
            emit("vector", "bn_stats", cols[:, base + 6:base + 12], hi_ap)
            emit("vector", "bn_aggr", cols[:, base + 12:base + 14],
                 cols[:, base + 0:base + 12])
            v = mark("vector", "pV")
            wait("scalar", "pV", v)
            emit("scalar", "activation", cols[:, base + 14:base + 15],
                 cols[:, base + 13:base + 14],
                 AF.Sqrt, bias=epsc[:, 0:1])
            s = mark("scalar", "pS")
            wait("vector", "pS", s)
            emit("vector", "reciprocal", cols[:, rc:rc + 1],
                 cols[:, base + 14:base + 15])
            emit("vector", "tensor_scalar", cols[:, ncol:ncol + 1],
                 cols[:, base + 12:base + 13],
                 cols[:, rc:rc + 1], -1.0, OP.mult, OP.mult)
            return mark("vector", "pV")

        def normalize(dst, src, vmark_val, rc=15, ncol=19):
            wait("scalar", "pV", vmark_val)
            emit("scalar", "activation", dst, src, AF.Identity,
                 bias=cols[:, ncol:ncol + 1], scale=cols[:, rc:rc + 1])
            return mark("scalar", "pS")

        tr_i = [0]

        def transpose_to(dst_ap, src_ap):
            b = 6 + (tr_i[0] % 2)
            tr_i[0] += 1
            open_banks(b)
            emit("tensor", "transpose", bank(b, 0, 128), src_ap, ident[:, :])
            t = mark("tensor", "pT")
            wait("scalar", "pT", t)
            emit("scalar", "activation", dst_ap, bank(b, 0, 128), AF.Copy)
            sm = mark("scalar", "pS")
            set_free([b], "pS", sm)
            return sm

        out_dmas = 0
        ln_busy_v = [0, 0]
        xlt_free = [0, 0]
        x1nt_free = 0
        c_tail = [0] * 4
        c_dma = [0] * 4
        x1nt_busy_v = 0
        xs_busy = [0, 0]

        for h in range(2):
            chunks = ([("warm", 0)] if h == 0 else []) + [("own", tc) for tc in range(NTC)]
            fstate = [None] * len(chunks)
            lstate = [None] * len(chunks)

            def ln_load(ci):
                kind, tc = chunks[ci]
                par = ci % 2
                if kind == "warm":
                    src = ap(dxw, 0, [[D, 128], [1, D]])
                else:
                    row0 = h * HALF + tc * 128
                    src = ap(dx, row0 * D, [[D, 128], [1, D]])
                wait("scalar", "pV", ln_busy_v[par])
                sname = "dX" if par == 0 else "dL"
                lstate[ci] = (sname,
                              dma("scalar", BUF4[:, XSLs[par]:XSLs[par] + D],
                                  src, sname))

            def ln_front(ci):
                kind, tc = chunks[ci]
                par = ci % 2
                XSL, XLT = XSLs[par], XLTs[par]
                OB = 32 * par
                wait("vector", lstate[ci][0], lstate[ci][1])
                vm = ln_stats(BUF4[:, XSL:XSL + 512], BUF4[:, XSL + 512:XSL + D],
                              rc=OB + 15, ncol=OB + 19, base=OB)
                if kind == "own":
                    xo = tc * D
                    sm = normalize(BUF0[:, xo:xo + D], BUF4[:, XSL:XSL + D], vm,
                                   rc=OB + 15, ncol=OB + 19)
                    xn_lo, xn_hi = BUF0[:, xo:xo + 512], BUF0[:, xo + 512:xo + D]
                    xn_full = BUF0[:, xo:xo + D]
                else:
                    wait("scalar", "pT", xlt_free[par])
                    sm = normalize(BUF4[:, XLT:XLT + D], BUF4[:, XSL:XSL + D], vm,
                                   rc=OB + 15, ncol=OB + 19)
                    xn_lo, xn_hi = BUF4[:, XLT:XLT + 512], BUF4[:, XLT + 512:XLT + D]
                    xn_full = BUF4[:, XLT:XLT + D]
                if kind == "own" and "xn" in TAPS:
                    wait("scalar", "pS", sm)
                    orow = h * HALF + tc * 128
                    dma("scalar", ap(dtap["xn"], orow * D, [[D, 128], [1, D]]),
                        xn_full, "dO")
                if kind == "own":
                    wait("scalar", "pS", sm)
                    dma("scalar", ap(dxnh, tc * 128 * D, [[D, 128], [1, D]]),
                        BUF0[:, tc * D:(tc + 1) * D], "dH")
                fstate[ci] = (kind, tc, par, XSL, XLT, sm, xn_lo, xn_hi, xn_full)

            def ln_back(ci):
                kind, tc, par, XSL, XLT, sm, xn_lo, xn_hi, xn_full = fstate[ci]
                IB = 64
                wait("vector", "pS", sm)
                vm2 = ln_stats(xn_lo, xn_hi, rc=IB + 15, ncol=IB + 19, base=IB)
                sm2 = normalize(BUF4[:, XSL:XSL + D], xn_full, vm2,
                                rc=IB + 15, ncol=IB + 19)
                wait("vector", "pS", sm2)
                wait("vector", "pT", xlt_free[par])
                emit("vector", "tensor_tensor", BUF4[:, XLT:XLT + D],
                     BUF4[:, XSL:XSL + D], lnw_r[:, :], OP.mult)
                emit("vector", "tensor_tensor", BUF4[:, XLT:XLT + D],
                     BUF4[:, XLT:XLT + D], lnb_r[:, :], OP.add)
                ln_busy_v[par] = mark("vector", "pV")
                if ci + 2 < len(chunks):
                    ln_load(ci + 2)
                wait("tensor", "pV", ln_busy_v[par])
                ev = 0
                for dg in range(DG):
                    if kind == "own":
                        dst = BUF1[:, dg * HALF + tc * 128:dg * HALF + tc * 128 + 128]
                    else:
                        dst = BUF3[:, dg * HALF:dg * HALF + 128]
                    ev = transpose_to(dst, BUF4[:, XLT + dg * 128:XLT + dg * 128 + 128])
                xlt_free[par] = CNT["pT"]
                return ev

            ln_load(0)
            if len(chunks) > 1:
                ln_load(1)
            ln_front(0)
            for ci in range(len(chunks)):
                if ci + 1 < len(chunks):
                    ln_front(ci + 1)
                xln_evac = ln_back(ci)
            xnh_done = CNT["dH"]
            ln_s_all = CNT["pS"]

            scan_q = []

            def gen_scan():
                xe = xln_evac
                for dg in range(DG):
                    if h == 0:
                        for n in range(NST):
                            l = dg * NST + n

                            def warm(l=l, dg=dg, xe=xe):
                                if l == 0:
                                    wait("vector", "pS", xe)
                                emit("vector", "tensor_scalar", BUF4[:, BXo:BXo + WARM],
                                     BUF3[:, dg * HALF:dg * HALF + WARM],
                                     Bsb[:, l:l + 1], maskc[:, 0:1], OP.mult, OP.mult)
                                emit("vector", "tensor_tensor_scan", BUF4[:, HSo:HSo + WARM],
                                     ap(Asig, l, [[128, 128], [0, WARM]]),
                                     BUF4[:, BXo:BXo + WARM],
                                     stcol[:, l:l + 1], OP.mult, OP.add)
                                emit("vector", "tensor_copy", stcol[:, l:l + 1],
                                     BUF4[:, HSo + WARM - 1:HSo + WARM])
                            scan_q.append(warm)
                    cdma_h1 = list(c_dma)

                    def dinit(dg=dg, xe=xe, cdma_h1=cdma_h1):
                        if h != 0 and dg == 0:
                            wait("vector", "pS", xe)
                            for i4 in range(4):
                                if cdma_h1[i4] > 0:
                                    wait("vector", "dC%d" % i4, cdma_h1[i4])
                        emit("vector", "tensor_scalar",
                             BUF3[:, dg * HALF:(dg + 1) * HALF],
                             BUF1[:, dg * HALF:(dg + 1) * HALF],
                             Dsb[:, dg:dg + 1], None, OP.mult)
                    scan_q.append(dinit)
                    for n in range(NST):
                        l = dg * NST + n
                        for run in range(4):
                            def runf(l=l, dg=dg, run=run):
                                o = run * 256
                                emit("vector", "tensor_scalar", BUF4[:, BXo:BXo + 256],
                                     BUF1[:, dg * HALF + o:dg * HALF + o + 256],
                                     Bsb[:, l:l + 1], None, OP.mult)
                                emit("vector", "tensor_tensor_scan", BUF4[:, HSo:HSo + 256],
                                     ap(Asig, l, [[128, 128], [0, 256]]),
                                     BUF4[:, BXo:BXo + 256],
                                     (stcol[:, l:l + 1] if run == 0 else
                                      BUF4[:, HSo + 255:HSo + 256]), OP.mult, OP.add)
                                if run == 3:
                                    emit("vector", "tensor_copy", stcol[:, l:l + 1],
                                         BUF4[:, HSo + 255:HSo + 256])
                                emit("vector", "scalar_tensor_tensor",
                                     BUF3[:, dg * HALF + o:dg * HALF + o + 256],
                                     BUF4[:, HSo:HSo + 256], Csb[:, l:l + 1],
                                     BUF3[:, dg * HALF + o:dg * HALF + o + 256],
                                     OP.mult, OP.add)
                            scan_q.append(runf)

            gen_scan()
            scan_pos = [0]

            def drain_scan(k):
                while k > 0 and scan_pos[0] < len(scan_q):
                    scan_q[scan_pos[0]]()
                    scan_pos[0] += 1
                    k -= 1

            RXF = 0
            for tc in range(NTC):
                wait("tensor", "pS", ln_s_all)
                if "b0" in TAPS and h == 1 and tc == 0:
                    wait("scalar", "pS", ln_s_all)
                    dma("scalar", dtap["b0"].ap(), BUF0[:, tc * D:(tc + 1) * D], "dO")
                    out_dmas += 1
                    wait("tensor", "dO", CNT["dO"])
                for dg in range(DG):
                    rxf_ev = transpose_to(BUF4[:, RXF + dg * 128:RXF + dg * 128 + 128],
                                          BUF0[:, tc * D + dg * 128:tc * D + dg * 128 + 128])
                wait("tensor", "pS", rxf_ev)
                if "rx" in TAPS and h == 1 and tc == 0:
                    wait("scalar", "pS", rxf_ev)
                    dma("scalar", dtap["rx"].ap(), BUF4[:, RXF:RXF + D], "dO")
                    out_dmas += 1
                    wait("tensor", "dO", CNT["dO"])
                open_banks(4)
                for kc in range(DG):
                    emit("tensor", "matmul", bank(4, 0, NEXP),
                         BUF4[:, RXF + kc * 128:RXF + kc * 128 + 128],
                         wr_sb[:, kc * NEXP:(kc + 1) * NEXP],
                         start=(kc == 0), stop=(kc == DG - 1))
                tm = mark("tensor", "pT")
                wait("scalar", "pT", tm)
                emit("scalar", "activation", r8a[:, :], bank(4, 0, NEXP),
                     AF.Identity, scale=sc32[:, 0:1])
                rt_evac = mark("scalar", "pS")
                set_free([4], "pS", rt_evac)
                if "lg" in TAPS:
                    dma("scalar", ap(dtap["lg"], (h * HALF + tc * 128) * NEXP,
                                     [[NEXP, 128], [1, NEXP]]), r8a[:, :], "dO")
                    out_dmas += 1
                    wait("vector", "dO", CNT["dO"])
                wait("vector", "pS", rt_evac)
                V = lambda *a, **k: emit("vector", *a, **k)
                V("tensor_tensor", r8b[:, :], r8a[:, :], rb_r[:, :], OP.add)
                V("tensor_reduce", cols[:, 16:17], r8b[:, :], AX.X, OP.max)
                V("tensor_scalar", r8c[:, :], r8b[:, :], cols[:, 16:17], None, OP.is_equal)
                V("tensor_scalar", r8d[:, :], r8c[:, :], -1.0, 1.0, OP.mult, OP.add)
                V("tensor_tensor", r8c[:, :], r8c[:, :], iota_r[:, :], OP.mult)
                V("scalar_tensor_tensor", r8c[:, :], r8d[:, :], BIG, r8c[:, :], OP.mult, OP.add)
                V("tensor_reduce", cols[:, 17:18], r8c[:, :], AX.X, OP.min)
                V("tensor_scalar", r8c[:, :], iota_r[:, :], cols[:, 17:18], None, OP.is_equal)
                V("scalar_tensor_tensor", r8b[:, :], r8c[:, :], -BIG, r8b[:, :], OP.mult, OP.add)
                V("tensor_reduce", cols[:, 16:17], r8b[:, :], AX.X, OP.max)
                V("tensor_scalar", r8d[:, :], r8b[:, :], cols[:, 16:17], None, OP.is_equal)
                V("tensor_scalar", r8b[:, :], r8d[:, :], -1.0, 1.0, OP.mult, OP.add)
                V("tensor_tensor", r8d[:, :], r8d[:, :], iota_r[:, :], OP.mult)
                V("scalar_tensor_tensor", r8d[:, :], r8b[:, :], BIG, r8d[:, :], OP.mult, OP.add)
                V("tensor_reduce", cols[:, 17:18], r8d[:, :], AX.X, OP.min)
                V("tensor_scalar", r8d[:, :], iota_r[:, :], cols[:, 17:18], None, OP.is_equal)
                V("tensor_copy", M1b[:, tc * NEXP:(tc + 1) * NEXP], r8c[:, :])
                V("tensor_copy", M2b[:, tc * NEXP:(tc + 1) * NEXP], r8d[:, :])
                V("tensor_tensor", r8c[:, :], r8c[:, :], r8d[:, :], OP.add)
                V("tensor_scalar", r8d[:, :], r8c[:, :], -1.0, 1.0, OP.mult, OP.add)
                V("scalar_tensor_tensor", r8d[:, :], r8d[:, :], -BIG, r8a[:, :], OP.mult, OP.add)
                V("tensor_reduce", cols[:, 16:17], r8d[:, :], AX.X, OP.max, negate=True)
                vm = mark("vector", "pV")
                wait("scalar", "pV", vm)
                emit("scalar", "activation", r8b[:, :], r8a[:, :], AF.Exp,
                     bias=cols[:, 16:17])
                sm2 = mark("scalar", "pS")
                wait("vector", "pS", sm2)
                V("tensor_tensor", r8b[:, :], r8b[:, :], r8c[:, :], OP.mult)
                V("reduce_sum", cols[:, 17:18], r8b[:, :], axis=AX.X)
                V("reciprocal", cols[:, 18:19], cols[:, 17:18])
                V("tensor_scalar", w_T[:, tc * NEXP:(tc + 1) * NEXP], r8b[:, :],
                  cols[:, 18:19], None, OP.mult)
                V("tensor_tensor", r8b[:, :], w_T[:, tc * NEXP:(tc + 1) * NEXP],
                  M1b[:, tc * NEXP:(tc + 1) * NEXP], OP.mult)
                V("reduce_sum", W1c[:, tc:tc + 1], r8b[:, :], axis=AX.X)
                V("tensor_tensor", r8b[:, :], w_T[:, tc * NEXP:(tc + 1) * NEXP],
                  M2b[:, tc * NEXP:(tc + 1) * NEXP], OP.mult)
                V("reduce_sum", W2c[:, tc:tc + 1], r8b[:, :], axis=AX.X)
                mark("vector", "pV")
            w_ready = CNT["pV"]
            if "w" in TAPS:
                wait("scalar", "pV", w_ready)
                dma("scalar", ap(dtap["w"], h * HALF * NEXP,
                                 [[NEXP, 128], [128 * NEXP, NTC], [1, NEXP]]),
                    w_T[:, :], "dO")
                out_dmas += 1

            sel_tr = 0
            for tc in (range(NTC) if not DBG_SKIP_IDX else []):
                wait("vector", "pT", sel_tr)
                emit("vector", "tensor_tensor", BUF4[:, 0:NEXP],
                     M1b[:, tc * NEXP:(tc + 1) * NEXP],
                     M2b[:, tc * NEXP:(tc + 1) * NEXP], OP.add)
                svm = mark("vector", "pV")
                wait("tensor", "pV", svm)
                b6 = 6 + (tc % 2)
                open_banks(b6)
                emit("tensor", "transpose", bank(b6, 0, 128),
                     BUF4[:, 0:128], ident[:, :])
                sel_tr = ttm = mark("tensor", "pT")
                wait("scalar", "pT", ttm)
                emit("scalar", "activation", BUF2[0:8, tc * 128:(tc + 1) * 128],
                     bass.AP(PS, b6 * 512, [[4096, 8], [1, 128]]), AF.Copy)
                set_free([b6], "pS", mark("scalar", "pS"))
            selT_s = CNT["pS"]
            if DBG_SKIP_IDX:
                emit("vector", "memset", BUF2[0:8, 0:2 * HALF], 0.0)
            wait("vector", "pS", selT_s)
            emit("vector", "tensor_tensor_scan", BUF2[0:8, HALF:2 * HALF],
                 bass.AP(ones_col, 0, [[1, 8], [0, HALF]]), BUF2[0:8, 0:HALF],
                 0.0, OP.mult, OP.add)
            emit("vector", "tensor_tensor", BUF2[0:8, HALF:2 * HALF],
                 BUF2[0:8, HALF:2 * HALF], BUF2[0:8, 0:HALF], OP.subtract)
            rank_v = mark("vector", "pV")
            rank_cons = 0
            scat_prev = 0
            for tc in (range(NTC) if not DBG_SKIP_IDX else []):
                b6 = 6 + (tc % 2)
                wait("tensor", "pV", rank_v)
                open_banks(b6)
                emit("tensor", "transpose", bank(b6, 0, 128),
                     bass.AP(BUF2, HALF + tc * 128, [[8192, 128], [1, 128]]),
                     ident[:, :])
                ttm = mark("tensor", "pT")
                wait("scalar", "pT", ttm)
                wait("scalar", "pV", rank_cons)
                emit("scalar", "activation", r8a[:, :],
                     bass.AP(PS, b6 * 512, [[4096, 128], [1, 8]]), AF.Copy)
                rsm = mark("scalar", "pS")
                set_free([b6], "pS", rsm)
                wait("vector", "pS", rsm)
                emit("vector", "tensor_scalar", r8b[:, :], r8a[:, :], float(CAP),
                     None, OP.is_lt)
                emit("vector", "tensor_tensor", r8d[:, :], r8a[:, :], ecr[:, :], OP.add)
                emit("vector", "tensor_tensor", r8d[:, :], r8d[:, :], r8b[:, :], OP.mult)
                emit("vector", "tensor_scalar", r8c[:, :], r8b[:, :], -1.0, 1.0,
                     OP.mult, OP.add)
                emit("vector", "scalar_tensor_tensor", r8d[:, :], r8c[:, :],
                     float(TRASH), r8d[:, :], OP.mult, OP.add)
                emit("vector", "tensor_tensor", r8c[:, :], r8d[:, :],
                     M1b[:, tc * NEXP:(tc + 1) * NEXP], OP.mult)
                emit("vector", "reduce_sum", cols[:, 23:24], r8c[:, :], axis=AX.X)
                emit("vector", "tensor_tensor", r8c[:, :], r8d[:, :],
                     M2b[:, tc * NEXP:(tc + 1) * NEXP], OP.mult)
                emit("vector", "reduce_sum", cols[:, 24:25], r8c[:, :], axis=AX.X)
                emit("vector", "tensor_copy", P1i[:, tc:tc + 1], cols[:, 23:24])
                emit("vector", "tensor_copy", P2i[:, tc:tc + 1], cols[:, 24:25])
                emit("vector", "tensor_scalar", cols[:, 25:26], partcol[:, 0:1],
                     float(tc * 128), None, OP.add)
                wait("vector", "dG", scat_prev)
                emit("vector", "tensor_copy", tok_i[:, 0:1], cols[:, 25:26])
                rank_cons = idx_v = mark("vector", "pV")
                if "ps" in TAPS:
                    wait("scalar", "pV", idx_v)
                    dma("scalar", ap(dtap["ps"], (h * HALF + tc * 128) * 2,
                                     [[2, 128], [1, 1]]), cols[:, 23:24], "dO",
                        allow_slow_non_contiguous=True)
                    dma("scalar", ap(dtap["ps"], (h * HALF + tc * 128) * 2 + 1,
                                     [[2, 128], [1, 1]]), cols[:, 24:25], "dO",
                        allow_slow_non_contiguous=True)
                    out_dmas += 2
                wait("gpsimd", "pV", idx_v)
                for Pi in (P1i, P2i):
                    wait("gpsimd", "dG", CNT["dG"])
                    emit("gpsimd", "indirect_dma_start",
                         ap(dstok, 0, [[1, 3328], [1, 1]]),
                         IndirectOffsetOnAxis(ap=Pi[:, tc:tc + 1], axis=0),
                         tok_i[:, 0:1], None)
                    mark("gpsimd", "dG", 16)
                scat_prev = CNT["dG"]
            wait("gpsimd", "dG", CNT["dG"])
            dma("gpsimd", st_sb[:, 0:26], ap(dstok, 0, [[1, 128], [128, 26]]), "dG",
                allow_slow_non_contiguous=True)
            st_ready = CNT["dG"]
            if "st" in TAPS and h == 0:
                wait("scalar", "dG", st_ready)
                emit("scalar", "activation", BUF4[:, 1024:1050], st_sb[:, :], AF.Copy)
                stm = mark("scalar", "pS")
                dma("scalar", dtap["st"].ap(), BUF4[:, 1024:1050], "dO")
                out_dmas += 1

            ACCB = BUF2
            XGs = [0, 4096]
            LND = [2560, 3584]
            SLB2 = 2048
            STG = 4608
            if "xn" in TAPS:
                wait("scalar", "dO", CNT["dO"])
            groups = [("sh", 0), ("sh", 1)] + [("ex", e) for e in range(NEXP)]
            if DBG_SKIP_EXPERTS:
                groups = []
                emit("gpsimd", "memset", BUF2[:, :], 0.0)
                acc_sh = [mark("gpsimd", "pG")] * NTC
            land_free = [0, 0]
            xg_rel = [0, 0]
            pre_land = {}

            def emit_land(gi2, t):
                gk2, gv2 = groups[gi2]
                ls = t % 2
                sname = "dG" if ls == 0 else "dGb"
                wait("gpsimd", "pT", land_free[ls])
                wait("gpsimd", "dH", xnh_done)
                if gk2 == "sh":
                    dma("gpsimd", BUF4[:, LND[ls]:LND[ls] + D],
                        ap(dxnh, (gv2 * 512 + t * 128) * D, [[D, 128], [1, D]]),
                        sname)
                else:
                    wait("gpsimd", sname, CNT[sname])
                    emit("gpsimd", "indirect_dma_start",
                         BUF4[:, LND[ls]:LND[ls] + D], None,
                         ap(dxnh, 0, [[D, HALF], [1, D]]),
                         IndirectOffsetOnAxis(ap=st_sb[:, gv2 * 3 + t:gv2 * 3 + t + 1], axis=0))
                    mark("gpsimd", sname, 16)
                return sname, CNT[sname]
            sl_cons = 0
            stg_rel = 0
            g_prev_eot = 0
            eo_ev = 0
            acc_sh = [0] * NTC
            gb_busy = 0
            for gi, (gk, gv) in enumerate(groups):
                ntc_g = 4 if gk == "sh" else 3
                NT = 128 * ntc_g
                xs = XGs[gi % 2]
                xg_tr = 0
                for t in range(ntc_g):
                    ls = t % 2
                    if (gi, t) in pre_land:
                        sname, gdm = pre_land.pop((gi, t))
                    else:
                        sname, gdm = emit_land(gi, t)
                    wait("tensor", sname, gdm)
                    wait("tensor", "pT", xg_rel[gi % 2])
                    for dg in range(DG):
                        xg_tr = transpose_to(BUF0[:, xs + dg * 512 + t * 128:
                                                  xs + dg * 512 + t * 128 + 128],
                                             BUF4[:, LND[ls] + dg * 128:LND[ls] + dg * 128 + 128])
                    land_free[ls] = CNT["pT"]
                for hk in range(16):
                    par = hk % 2
                    if gk == "sh":
                        w1src = ap(dWs1, hk * 128, [[HE, 128], [128 * HE, DG], [1, 128]])
                        w3src = ap(dWs3, hk * 128, [[HE, 128], [128 * HE, DG], [1, 128]])
                    else:
                        base = gv * D * HE
                        w1src = ap(dW1, base + hk * 128, [[HE, 128], [128 * HE, DG], [1, 128]])
                        w3src = ap(dW3, base + hk * 128, [[HE, 128], [128 * HE, DG], [1, 128]])
                    wv1 = stream_slot(0, w1src)
                    wv3 = stream_slot(1, w3src)
                    wait("tensor", "ws0", wv1)
                    if hk == 0:
                        wait("tensor", "pS", xg_tr)
                    open_banks(par)
                    for kc in range(DG):
                        emit("tensor", "matmul", bank(par, 0, NT),
                             BUFY[:, kc * 128:kc * 128 + 128],
                             BUF0[:, xs + kc * 512:xs + kc * 512 + NT],
                             start=(kc == 0), stop=(kc == DG - 1))
                    yrel[0] = h1t = mark("tensor", "pT")
                    wait("tensor", "ws1", wv3)
                    open_banks(2 + par)
                    for kc in range(DG):
                        emit("tensor", "matmul", bank(2 + par, 0, NT),
                             BUFY[:, 1024 + kc * 128:1024 + kc * 128 + 128],
                             BUF0[:, xs + kc * 512:xs + kc * 512 + NT],
                             start=(kc == 0), stop=(kc == DG - 1))
                    yrel[1] = h3t = mark("tensor", "pT")
                    wait("scalar", "pT", h3t)
                    if sl_cons > 0:
                        wait("scalar", "pV", sl_cons)
                    emit("scalar", "activation", BUF4[:, SLB2:SLB2 + NT],
                         bank(2 + par, 0, NT), AF.Silu)
                    slm = mark("scalar", "pS")
                    set_free([2 + par], "pS", slm)
                    wait("vector", "pS", slm)
                    wait("vector", "pT", h1t)
                    if g_prev_eot > 0:
                        wait("vector", "pT", g_prev_eot)
                    emit("vector", "tensor_tensor",
                         BUF4[:, GQ + (hk % 4) * 512:GQ + (hk % 4) * 512 + NT],
                         bank(par, 0, NT), BUF4[:, SLB2:SLB2 + NT], OP.mult)
                    sl_cons = g_last = mark("vector", "pV")
                    set_free([par], "pV", g_last)
                    drain_scan(4)
                    if hk % 4 == 3:
                        q = hk // 4
                        w2w = [0] * 4
                        for i in range(4):
                            hk2 = q * 4 + i
                            if gk == "sh":
                                src = ap(dWs2, hk2 * 128 * D, [[D, 128], [1, D]])
                            else:
                                src = ap(dW2, gv * HE * D + hk2 * 128 * D, [[D, 128], [1, D]])
                            w2w[i] = stream_slot(2 + i, src)
                        if q == 3 and gi + 1 < len(groups):
                            for t2 in range(2):
                                pre_land[(gi + 1, t2)] = emit_land(gi + 1, t2)
                        for t in range(ntc_g):
                            pb = 4 + (t % 2) * 2
                            wait("tensor", "pV", g_last)
                            open_banks(pb, pb + 1)
                            for i in range(4):
                                wait("tensor", "ws%d" % (2 + i), w2w[i])
                                for dh in range(2):
                                    emit("tensor", "matmul", bank(pb + dh),
                                         BUF4[:, GQ + i * 512 + t * 128:GQ + i * 512 + t * 128 + 128],
                                         BUFY[:, (2 + i) * 1024 + dh * 512:(2 + i) * 1024 + dh * 512 + 512],
                                         start=(i == 0), stop=(i == 3))
                            g_prev_eot = eot = mark("tensor", "pT")
                            if gk == "sh":
                                tc = gv * 4 + t
                                wait("vector", "pT", eot)
                                if q == 0:
                                    emit("vector", "tensor_copy",
                                         ACCB[:, tc * D:(tc + 1) * D], bank2(pb))
                                else:
                                    emit("vector", "tensor_tensor",
                                         ACCB[:, tc * D:(tc + 1) * D], bank2(pb),
                                         ACCB[:, tc * D:(tc + 1) * D], OP.add)
                                eo_ev = mark("vector", "pV")
                                if q == 3:
                                    acc_sh[tc] = eo_ev
                                set_free([pb, pb + 1], "pV", eo_ev)
                            else:
                                wait("scalar", "pT", eot)
                                if stg_rel > 0:
                                    wait("scalar", "dE", stg_rel)
                                emit("scalar", "activation", BUF4[:, STG:STG + D],
                                     bank2(pb), AF.Copy)
                                ssm = mark("scalar", "pS")
                                set_free([pb, pb + 1], "pS", ssm)
                                row0 = gv * CAP + t * 128
                                wait("gpsimd", "pS", ssm)
                                dma("gpsimd", ap(deog, row0 * D, [[D, 128], [1, D]]),
                                    BUF4[:, STG:STG + D], "dE",
                                    **({} if q == 0 else {"accum_op": OP.add}))
                                stg_rel = CNT["dE"]
                            drain_scan(3)
                xg_rel[gi % 2] = h3t
            eog_done = CNT["dE"]
            b_eot = g_prev_eot
            drain_scan(10 ** 9)
            scan_done = CNT["pV"]

            wait("gpsimd", "dE", eog_done)
            gbsem = ["dG", "dGb", "dG2", "dG3"]
            gb_pair_busy = [0, 0]
            for tc in (range(NTC) if not DBG_SKIP_EXPERTS else []):
                pr = tc % 2
                for k, Pi in ((0, P1i), (1, P2i)):
                    ls = pr * 2 + k
                    if gb_pair_busy[pr] > 0:
                        wait("gpsimd", "pV", gb_pair_busy[pr])
                    wait("gpsimd", gbsem[ls], CNT[gbsem[ls]])
                    emit("gpsimd", "indirect_dma_start",
                         BUF0[:, ls * 1024:ls * 1024 + D], None,
                         ap(deog, 0, [[D, EOGR], [1, D]]),
                         IndirectOffsetOnAxis(ap=Pi[:, tc:tc + 1], axis=0))
                    mark("gpsimd", gbsem[ls], 16)
                wait("vector", gbsem[pr * 2], CNT[gbsem[pr * 2]])
                wait("vector", gbsem[pr * 2 + 1], CNT[gbsem[pr * 2 + 1]])
                wait("vector", "pV", acc_sh[tc])
                emit("vector", "scalar_tensor_tensor",
                     ACCB[:, tc * D:(tc + 1) * D], BUF0[:, pr * 2048:pr * 2048 + D],
                     W1c[:, tc:tc + 1], ACCB[:, tc * D:(tc + 1) * D],
                     OP.mult, OP.add)
                emit("vector", "scalar_tensor_tensor",
                     ACCB[:, tc * D:(tc + 1) * D],
                     BUF0[:, pr * 2048 + 1024:pr * 2048 + 1024 + D],
                     W2c[:, tc:tc + 1], ACCB[:, tc * D:(tc + 1) * D],
                     OP.mult, OP.add)
                gb_pair_busy[pr] = mark("vector", "pV")
            acc_done = CNT["pV"]
            if DBG_SKIP_EXPERTS:
                b_eot = CNT["pT"]
            if "acc" in TAPS:
                wait("scalar", "pV", acc_done)
                for tc in range(NTC):
                    orow = h * HALF + tc * 128
                    dma("scalar", ap(dtap["acc"], orow * D, [[D, 128], [1, D]]),
                        ACCB[:, tc * D:(tc + 1) * D], "dO")
                    out_dmas += 1
            if "y" in TAPS:
                wait("scalar", "pV", scan_done)
                for dg in range(DG):
                    dma("scalar", ap(dtap["y"], (h * HALF) * D + dg * 128,
                                     [[1, 128], [D, HALF]]),
                        BUF3[:, dg * HALF:(dg + 1) * HALF], "dO",
                        allow_slow_non_contiguous=True)
                    out_dmas += 1

            s_evac_last = 0
            prev_mmdone = 0
            for tsup in range(4):
                toff = tsup * 256
                tc0 = tsup * 2
                if tsup == 0:
                    wait("scalar", "pV", scan_done)
                    for dg in range(DG):
                        emit("scalar", "activation",
                             BUF3[:, dg * HALF:(dg + 1) * HALF],
                             BUF3[:, dg * HALF:(dg + 1) * HALF], AF.Silu)
                ysil = mark("scalar", "pS")
                if "ys" in TAPS:
                    wait("scalar", "pS", ysil)
                    for dg in range(DG):
                        dma("scalar", ap(dtap["ys"], (h * HALF + toff) * D + dg * 128,
                                         [[1, 128], [D, 256]]),
                            BUF4[:, YSIL + dg * 256:YSIL + dg * 256 + 256], "dO",
                            allow_slow_non_contiguous=True)
                        out_dmas += 1
                    wait("tensor", "dO", CNT["dO"])
                bb = (tsup % 2) * 4
                rcs, ncs = [15, 21], [19, 22]
                for tcoff in range(2):
                    tc = tc0 + tcoff
                    xs = tcoff * 1024
                    wait("scalar", "pV", xs_busy[tcoff])
                    wait("scalar", "pT", b_eot)
                    row0 = h * HALF + tc * 128
                    dvx = dma("scalar", BUF4[:, xs:xs + D],
                              ap(dx, row0 * D, [[D, 128], [1, D]]), "dX")
                    wait("vector", "dX", dvx)
                    ln_stats(BUF4[:, xs:xs + 512], BUF4[:, xs + 512:xs + D],
                             rc=rcs[tcoff], ncol=ncs[tcoff])
                for k in range(DG):
                    slot = k % 2
                    wv = stream_slot(slot, ap(doutwT, k * 128 * D, [[D, 128], [1, D]]))
                    wait("tensor", "ws%d" % slot, wv)
                    if k == 0:
                        wait("tensor", "pS", ysil)
                        open_banks(bb, bb + 1, bb + 2, bb + 3)
                        for tcoff in range(2):
                            for oh in range(2):
                                if DBG_SKIP_BIAS:
                                    continue
                                emit("tensor", "matmul", bank(bb + tcoff * 2 + oh),
                                     ones_row[0:1, 0:128],
                                     outb_row[0:1, oh * 512:oh * 512 + 512],
                                     start=True, stop=False)
                    for tcoff in range(2):
                        for oh in range(2):
                            emit("tensor", "matmul", bank(bb + tcoff * 2 + oh),
                                 BUF3[:, k * HALF + toff + tcoff * 128:
                                      k * HALF + toff + tcoff * 128 + 128],
                                 BUFY[:, slot * 1024 + oh * 512:
                                      slot * 1024 + oh * 512 + 512],
                                 start=(DBG_SKIP_BIAS and k == 0),
                                 stop=(k == DG - 1))
                    yrel[slot] = mark("tensor", "pT")
                prev_mmdone = mmdone = CNT["pT"]
                x1m = [0, 0]
                for tcoff in range(2):
                    tc = tc0 + tcoff
                    xs = tcoff * 1024
                    wait("vector", "pT", mmdone)
                    if "x1" in TAPS:
                        wait("vector", "dO", CNT["dO"])
                    emit("vector", "scalar_tensor_tensor",
                         BUF0[:, tc * D:(tc + 1) * D],
                         BUF4[:, xs:xs + D], cols[:, rcs[tcoff]:rcs[tcoff] + 1],
                         bank2(bb + tcoff * 2), OP.mult, OP.add)
                    emit("vector", "tensor_scalar",
                         BUF0[:, tc * D:(tc + 1) * D],
                         BUF0[:, tc * D:(tc + 1) * D],
                         cols[:, ncs[tcoff]:ncs[tcoff] + 1], None, OP.add)
                    x1m[tcoff] = xs_busy[tcoff] = mark("vector", "pV")
                    if "x1" in TAPS:
                        wait("scalar", "pV", x1m[tcoff])
                        orow = h * HALF + tc * 128
                        dma("scalar", ap(dtap["x1"], orow * D, [[D, 128], [1, D]]),
                            BUF0[:, tc * D:(tc + 1) * D], "dO")
                        out_dmas += 1
                set_free([bb, bb + 1], "pV", x1m[0])
                set_free([bb + 2, bb + 3], "pV", x1m[1])
            x1_all = CNT["pV"]

            x1n_ready_t = [0] * 4
            gel = [0, 0]
            f2rel = [0, 0]
            sdone_t = [0] * 4
            for step in range(6):
                if step < 4:
                    tsup = step
                    tc0 = tsup * 2
                    xpar = tsup % 2
                    X1NF = X1NFs[xpar]
                    for tcoff in range(2):
                        tc = tc0 + tcoff
                        x1a = BUF0[:, tc * D:(tc + 1) * D]
                        wait("vector", "pV", x1_all)
                        vm = ln_stats(BUF0[:, tc * D:tc * D + 512],
                                      BUF0[:, tc * D + 512:(tc + 1) * D])
                        wait("scalar", "pT", x1nt_free)
                        wait("scalar", "pV", x1nt_busy_v)
                        sm = normalize(BUF4[:, X1NT:X1NT + D], x1a, vm)
                        wait("vector", "pS", sm)
                        emit("vector", "tensor_tensor", BUF4[:, X1NT:X1NT + D],
                             BUF4[:, X1NT:X1NT + D], ln2w_r[:, :], OP.mult)
                        emit("vector", "tensor_tensor", BUF4[:, X1NT:X1NT + D],
                             BUF4[:, X1NT:X1NT + D], ln2b_r[:, :], OP.add)
                        va = x1nt_busy_v = mark("vector", "pV")
                        wait("tensor", "pV", va)
                        for dg in range(DG):
                            o = X1NF + dg * 256 + tcoff * 128
                            x1n_ready_t[tsup] = transpose_to(
                                BUF4[:, o:o + 128],
                                BUF4[:, X1NT + dg * 128:X1NT + dg * 128 + 128])
                        x1nt_free = CNT["pT"]
                if 0 <= step - 2 <= 3:
                    tsup = step - 2
                    tc0 = tsup * 2
                    sdone = sdone_t[tsup]
                    for tcoff in range(2):
                        tc = tc0 + tcoff
                        wait("vector", "pT", sdone)
                        if tsup == 0 and tcoff == 0:
                            wait("vector", "pV", scan_done)
                        emit("vector", "tensor_tensor", BUF1[:, tc * D:(tc + 1) * D],
                             bank2(tcoff * 2),
                             BUF0[:, tc * D:(tc + 1) * D], OP.add)
                        s_evac_last = mark("vector", "pV")
                        if "s" in TAPS:
                            wait("scalar", "pV", s_evac_last)
                            orow = h * HALF + tc * 128
                            dma("scalar", ap(dtap["s"], orow * D, [[D, 128], [1, D]]),
                                BUF1[:, tc * D:(tc + 1) * D], "dO")
                            out_dmas += 1
                    set_free([0, 1], "pV", s_evac_last)
                    set_free([2, 3], "pV", s_evac_last)
                    for tcoff in range(2):
                        tc = tc0 + tcoff
                        sl4 = tc % 4
                        csem = "dC%d" % sl4
                        xre = sl4 * 1024
                        gg = 4096 + sl4 * 1024
                        wait("scalar", "pV", max(acc_done, c_tail[sl4], s_evac_last))
                        if c_dma[sl4] > 0:
                            wait("scalar", csem, c_dma[sl4])
                        if TAPS:
                            wait("scalar", "dO", CNT["dO"])
                            wait("vector", "dO", CNT["dO"])
                        emit("scalar", "activation", BUF3[:, gg:gg + D],
                             BUF2[:, tc * D:(tc + 1) * D], AF.Silu)
                        gsm = mark("scalar", "pS")
                        row0 = h * HALF + tc * 128
                        rsem = "dR%d" % tcoff
                        dv = dma("scalar", BUF3[:, xre:xre + D],
                                 ap(dx, row0 * D, [[D, 128], [1, D]]), rsem)
                        wait("vector", "pS", gsm)
                        wait("vector", rsem, dv)
                        wait("vector", "pV", s_evac_last)
                        if c_dma[sl4] > 0:
                            wait("vector", csem, c_dma[sl4])
                        emit("vector", "tensor_tensor", BUF3[:, gg:gg + D],
                             BUF1[:, tc * D:(tc + 1) * D], BUF3[:, gg:gg + D], OP.mult)
                        emit("vector", "tensor_tensor", BUF3[:, xre:xre + D],
                             BUF3[:, xre:xre + D], BUF3[:, gg:gg + D], OP.add)
                        c_tail[sl4] = mark("vector", "pV")
                        wait("gpsimd", "pV", c_tail[sl4])
                        c_dma[sl4] = dma("gpsimd", ap(dout, row0 * D, [[D, 128], [1, D]]),
                                         BUF3[:, xre:xre + D], csem)
                        out_dmas += 1
                if 1 <= step <= 4:
                    tsup = step - 1
                    tc0 = tsup * 2
                    xpar = tsup % 2
                    X1NF = X1NFs[xpar]
                    for hk in range(32):
                        par = hk % 2
                        w1 = stream_slot(2 + par,
                                         ap(dffn1T, hk * 128, [[H2, 128], [128 * H2, DG], [1, 128]]))
                        w2 = stream_slot(4 + par,
                                         ap(dffn2T, hk * 128 * D, [[D, 128], [1, D]]))
                        wait("tensor", "ws%d" % (2 + par), w1)
                        if hk == 0:
                            wait("tensor", "pS", x1n_ready_t[tsup])
                        open_banks(4 + par)
                        for kc in range(DG):
                            emit("tensor", "matmul", bank(4 + par, 0, 256),
                                 BUFY[:, (2 + par) * 1024 + kc * 128:
                                      (2 + par) * 1024 + kc * 128 + 128],
                                 BUF4[:, X1NF + kc * 256:X1NF + kc * 256 + 256],
                                 start=(kc == 0), stop=(kc == DG - 1))
                        yrel[2 + par] = f1t = mark("tensor", "pT")
                        if tsup == 3:
                            a2_f1_last = f1t
                        wait("scalar", "pT", f1t)
                        if tsup == 0 and hk == 0:
                            wait("scalar", "dE", eog_done)
                        if gel[par] > 0:
                            wait("scalar", "pT", f2rel[par])
                        emit("scalar", "activation",
                             BUF4[:, H2R + par * 256:H2R + par * 256 + 256],
                             bank(4 + par, 0, 256), AF.Gelu, bias=ffn1b_c[:, hk:hk + 1])
                        gel[par] = mark("scalar", "pS")
                        set_free([4 + par], "pS", gel[par])
                        wait("tensor", "ws%d" % (4 + par), w2)
                        wait("tensor", "pS", gel[par])
                        if hk == 0:
                            open_banks(0, 1, 2, 3)
                            for tcoff in range(2):
                                for dh in range(2):
                                    if DBG_SKIP_BIAS:
                                        continue
                                    emit("tensor", "matmul", bank(tcoff * 2 + dh),
                                         ones_row[0:1, 0:128],
                                         ffn2b_row[0:1, dh * 512:dh * 512 + 512],
                                         start=True, stop=False)
                        for tcoff in range(2):
                            for dh in range(2):
                                emit("tensor", "matmul", bank(tcoff * 2 + dh),
                                     BUF4[:, H2R + par * 256 + tcoff * 128:
                                          H2R + par * 256 + tcoff * 128 + 128],
                                     BUFY[:, (4 + par) * 1024 + dh * 512:
                                          (4 + par) * 1024 + dh * 512 + 512],
                                     start=(DBG_SKIP_BIAS and hk == 0),
                                     stop=(hk == 31))
                        yrel[4 + par] = f2rel[par] = mark("tensor", "pT")
                    sdone_t[tsup] = CNT["pT"]

            ln_busy_v = [xs_busy[0], xs_busy[1]]
            xlt_free = [x1nt_free, a2_f1_last]

        for cs in ("dC0", "dC1", "dC2", "dC3"):
            wait("scalar", cs, CNT[cs])
        if CNT["dO"] > 0:
            wait("scalar", "dO", CNT["dO"])

        block = ES.enter_context(nc.Block())

        def mk_replay(q, eng):
            def replay(e):
                for idx, item in enumerate(q):
                    try:
                        if item[0] == "wait":
                            e.wait_ge(item[1], item[2])
                        else:
                            _, meth, args, kw, incs = item
                            inst = getattr(e, meth)(*args, **kw)
                            for s, a in incs:
                                inst.then_inc(s, a)
                    except Exception:
                        import sys as _s
                        print("REPLAY FAIL eng=%s idx=%d/%d" % (eng, idx, len(q)),
                              file=_s.stderr)
                        for j in range(max(0, idx - 6), idx + 1):
                            it = q[j]
                            print("  q[%d]: %s" % (j, it[1] if it[0] == "op" else ("wait", it[2])),
                                  file=_s.stderr)
                        raise
            return replay

        for eng in ("tensor", "vector", "scalar", "gpsimd", "sync"):
            getattr(block, eng)(mk_replay(Q[eng], eng))
    return nc


def kernel(**inputs):
    x = np.ascontiguousarray(np.asarray(inputs["L0_attn_x"], dtype=np.float32))
    B, S, _ = x.shape
    per = S // 4

    w = {
        "ln_w": np.ascontiguousarray(inputs["L0_attn_ln_w"], np.float32),
        "ln_b": np.ascontiguousarray(inputs["L0_attn_ln_b"], np.float32),
        "ln2_w": np.ascontiguousarray(inputs["L0_attn_ln2_w"], np.float32),
        "ln2_b": np.ascontiguousarray(inputs["L0_attn_ln2_b"], np.float32),
        "A_log": np.ascontiguousarray(inputs["L0_attn_A_log"], np.float32),
        "B_mat": np.ascontiguousarray(inputs["L0_attn_B_mat"], np.float32),
        "C_mat": np.ascontiguousarray(inputs["L0_attn_C_mat"], np.float32),
        "D_vec": np.ascontiguousarray(inputs["L0_attn_D_vec"], np.float32),
        "outwT": np.ascontiguousarray(np.asarray(inputs["L0_attn_out_w"], dtype=np.float32).T),
        "out_b": np.ascontiguousarray(inputs["L0_attn_out_b"], np.float32),
        "ffn1T": np.ascontiguousarray(np.asarray(inputs["L0_attn_ffn1_w"], dtype=np.float32).T),
        "ffn1_b": np.ascontiguousarray(inputs["L0_attn_ffn1_b"], np.float32),
        "ffn2T": np.ascontiguousarray(np.asarray(inputs["L0_attn_ffn2_w"], dtype=np.float32).T),
        "ffn2_b": np.ascontiguousarray(inputs["L0_attn_ffn2_b"], np.float32),
        "Wr": np.ascontiguousarray(inputs["L0_ffn_Wr"], np.float32),
        "rb": np.ascontiguousarray(inputs["L0_ffn_rb"], np.float32),
        "W1": np.ascontiguousarray(inputs["L0_ffn_W1"], np.float32),
        "W2": np.ascontiguousarray(inputs["L0_ffn_W2"], np.float32),
        "W3": np.ascontiguousarray(inputs["L0_ffn_W3"], np.float32),
        "Ws1": np.ascontiguousarray(inputs["L0_ffn_Ws1"], np.float32),
        "Ws2": np.ascontiguousarray(inputs["L0_ffn_Ws2"], np.float32),
        "Ws3": np.ascontiguousarray(inputs["L0_ffn_Ws3"], np.float32),
        "c_ident": np.eye(128, dtype=np.float32),
        "c_iota": np.arange(NEXP, dtype=np.float32),
    }
    zeros_w = np.zeros((WARM, D), np.float32)
    in_maps = []
    for c in range(8):
        b, k = c // 4, c % 4
        q0 = k * per
        m = dict(w)
        m["x"] = np.ascontiguousarray(x[b, q0:q0 + per])
        m["xw"] = (zeros_w if k == 0 else
                   np.ascontiguousarray(x[b, q0 - WARM:q0]))
        m["mask"] = (np.zeros((1, 1), np.float32) if k == 0 else
                     np.ones((1, 1), np.float32))
        in_maps.append(m)

    nc = build_program()
    res = run_bass_kernel_spmd(nc, in_maps, core_ids=list(range(8)))
    out = np.empty((B, S, D), np.float32)
    for c in range(8):
        b, k = c // 4, c % 4
        out[b, k * per:(k + 1) * per] = res.results[c]["out"]
    return out

